# version 34 (final)
# baseline (speedup 1.0000x reference)
import os
import numpy as np
import concourse.bass as bass
import concourse.mybir as mybir
from concourse.bass_utils import run_bass_kernel_spmd

F32 = mybir.dt.float32
BF16 = mybir.dt.bfloat16
AF = mybir.ActivationFunctionType
ALU = mybir.AluOpType
AX = mybir.AxisListType

S = 4096
D = 1024
T = 128
NT = S // T
NH = 16
HD = 64
EPS = 1e-5

ENGINES = ("pe", "dve", "act", "pool", "sp")
SELF_WAIT = {"dve", "act", "pool", "sp"}


class Buf:
    __slots__ = ("name", "writers", "readers")

    def __init__(self, name):
        self.name = name
        self.writers = {}
        self.readers = {}


class Sched:
    def __init__(self):
        self.streams = {e: [] for e in ENGINES}
        self.cnt = {e: 0 for e in ENGINES}
        self.step = {e: (16 if e == "sp" else 1) for e in ENGINES}
        self.lastw = {}

    def emit(self, eng, fn, reads=(), writes=(), deps=()):
        need = {}

        def add(e, c):
            need[e] = max(need.get(e, 0), c)

        for b in reads:
            for e, c in b.writers.items():
                add(e, c)
        for b in writes:
            for e, c in b.writers.items():
                add(e, c)
            for e, c in b.readers.items():
                add(e, c)
        for e, c in deps:
            add(e, c)
        waits = []
        for e, c in sorted(need.items()):
            if e == eng and eng not in SELF_WAIT:
                continue
            k = (eng, e)
            if self.lastw.get(k, 0) >= c:
                continue
            self.lastw[k] = c
            waits.append((e, c * self.step[e]))
        self.cnt[eng] += 1
        me = self.cnt[eng]
        self.streams[eng].append((waits, fn))
        for b in reads:
            b.readers[eng] = max(b.readers.get(eng, 0), me)
        for b in writes:
            b.writers[eng] = max(b.writers.get(eng, 0), me)
        return (eng, me)


DBG = int(os.environ.get("KDBG", "0"))
DT = 2


def build_nc():
    nc = bass.Bass("TRN2", target_bir_lowering=False, dynamic_dma_scratch_size=4096)
    dbg = {}

    def dbg_out(name, shape, dtype):
        dbg[name] = nc.dram_tensor("dbg_" + name, shape, dtype,
                                   kind="ExternalOutput").ap()
        return dbg[name]

    xd = nc.dram_tensor("x", [S, D], F32, kind="ExternalInput").ap()
    outd = nc.dram_tensor("out", [S, D], F32, kind="ExternalOutput").ap()
    wdram = {}
    for nm in ("wq", "wk", "wv", "wo", "wf"):
        wdram[nm] = nc.dram_tensor(nm + "_t", [D, D], F32, kind="ExternalInput").ap()
    vr_d = nc.dram_tensor("vr", [D, 32], F32, kind="ExternalInput").ap()
    ct_d = nc.dram_tensor("ct", [32, D], F32, kind="ExternalInput").ap()
    rows_d = {}
    for nm in ("bq", "bk", "bv", "bo", "bf", "lnqw", "lnqb", "lnkw"):
        rows_d[nm] = nc.dram_tensor("r_" + nm, [1, D], F32, kind="ExternalInput").ap()

    s = Sched()
    E = s.emit

    def sb(name, shape, dtype):
        return nc.alloc_sbuf_tensor(name, shape, dtype)

    w_bf = {nm: sb(nm + "_bf", [128, 8 * D], BF16)
            for nm in ("wq", "wk", "wv", "wo", "wf")}
    vr_bf = sb("vr_bf", [128, 8 * 32], BF16)
    ct_bf = sb("ct_bf", [32, D], BF16)
    row_bf = {nm: sb("row_" + nm, [1, D], BF16)
              for nm in ("bq", "bk", "bv", "bo", "bf")}
    lnqw_rep = sb("lnqw_rep", [128, D], F32)
    lnqb_rep = sb("lnqb_rep", [128, D], F32)
    lnkw_rep = sb("lnkw_rep", [128, D], F32)
    idb = sb("idb", [128, 128], BF16)
    mcomb = sb("mcomb", [128, 256], BF16)
    ones1f = sb("ones1f", [1, 128], F32)
    ones1b = sb("ones1b", [1, 128], BF16)
    negB = sb("negB", [128, 1], F32)
    zero1 = sb("zero1", [128, 1], F32)
    eps5 = sb("eps5", [128, 1], F32)
    smallB = sb("smallB", [1, 8], F32)
    sq64 = sb("sq64", [1, 64], F32)

    x_ring = sb("x_ring", [128, 5 * D], F32)
    kT_ring = sb("kT_ring", [128, 4 * D], BF16)
    vaug_ring = sb("vaug_ring", [128, 4 * NH * 65], BF16)
    qT = sb("qT", [128, 2 * D], BF16)
    xnT = sb("xnT", [128, 2 * D], BF16)
    xn2T = sb("xn2T", [128, 2 * D], BF16)
    attT = sb("attT", [128, 2 * D], BF16)
    q_tm = sb("q_tm", [128, 2 * D], F32)
    k_tm = sb("k_tm", [128, 2 * D], F32)
    qh_bf = sb("qh_bf", [128, 2 * D], BF16)
    kh_bf = sb("kh_bf", [128, 2 * D], BF16)
    xn_bf = sb("xn_bf", [128, 2 * D], BF16)
    xn2_bf = sb("xn2_bf", [128, 2 * D], BF16)
    att_bf = sb("att_bf", [128, 2 * D], BF16)
    x2 = sb("x2", [128, 2 * D], F32)
    out_b = x2
    sq_buf = sb("sq_buf", [128, D], F32)
    sq_buf2 = sb("sq_buf2", [128, D], F32)
    probsT = sb("probsT", [128, 2 * 768], BF16)
    t1b = sb("t1b", [32, 2 * 128], BF16)
    stq = sb("stq", [128, 2 * 64], F32)
    stk = sb("stk", [128, 2 * 64], F32)
    stl = sb("stl", [128, 2 * 16], F32)
    rq = sb("rq", [128, 2 * 16], F32)
    rqt = sb("rqt", [128, 2 * 16], F32)

    wstage = x_ring
    rowtmp = out_b

    Pqkv = nc.alloc_psum_tensor("Pqkv", [128, 1536], F32)
    Pa = nc.alloc_psum_tensor("Pa", [128, 512], F32)
    Pw = nc.alloc_psum_tensor("Pw", [128, 512], F32)
    PtrA = nc.alloc_psum_tensor("PtrA", [128, 1024], BF16)
    PtrB = nc.alloc_psum_tensor("PtrB", [128, 1024], BF16)
    Pout = nc.alloc_psum_tensor("Pout", [128, 512], F32)

    b_w = {nm: Buf("w_" + nm) for nm in w_bf}
    b_vr, b_ct = Buf("vr"), Buf("ct")
    b_row = {nm: Buf("row_" + nm) for nm in row_bf}
    b_lnrep = {"lnqw": Buf("lnqw_rep"), "lnqb": Buf("lnqb_rep"),
               "lnkw": Buf("lnkw_rep")}
    b_idb, b_mask = Buf("idb"), Buf("mask")
    b_ones1f, b_ones1b = Buf("ones1f"), Buf("ones1b")
    b_negB, b_zero1, b_eps5 = Buf("negB"), Buf("zero1"), Buf("eps5")
    b_smallB, b_sq64 = Buf("smallB"), Buf("sq64")
    b_xr = [Buf(f"x{i}") for i in range(5)]
    b_kT = [Buf(f"kT{i}") for i in range(4)]
    b_va = [Buf(f"va{i}") for i in range(4)]
    b_qT = [Buf(f"qT{i}") for i in range(2)]
    b_xnT = [Buf(f"xnT{i}") for i in range(2)]
    b_xn2T = [Buf(f"xn2T{i}") for i in range(2)]
    b_attT = [Buf(f"attT{i}") for i in range(2)]
    b_qtm = [Buf(f"qtm{i}") for i in range(2)]
    b_ktm = [Buf(f"ktm{i}") for i in range(2)]
    b_qhb = [Buf(f"qhb{i}") for i in range(2)]
    b_khb = [Buf(f"khb{i}") for i in range(2)]
    b_xnb = [Buf(f"xnb{i}") for i in range(2)]
    b_xn2b = [Buf(f"xn2b{i}") for i in range(2)]
    b_atb = [Buf(f"atb{i}") for i in range(2)]
    b_x2 = [Buf(f"x2_{i}") for i in range(2)]
    b_out = b_x2
    b_sq = Buf("sq_buf")
    b_sq2 = Buf("sq_buf2")
    b_pT = [Buf(f"pT{i}") for i in range(2)]
    b_t1b = [Buf(f"t1b{i}") for i in range(2)]
    b_stq = [Buf(f"stq{i}") for i in range(2)]
    b_stk = [Buf(f"stk{i}") for i in range(2)]
    b_stl = [Buf(f"stl{i}") for i in range(2)]
    b_stl2 = [Buf(f"stl2_{i}") for i in range(2)]
    b_rq = [Buf(f"rq{i}") for i in range(2)]
    b_rqt = [Buf(f"rqt{i}") for i in range(2)]
    b_bank = [Buf(f"bank{i}") for i in range(8)]
    b_wstage = b_xr[0]
    b_rowtmp = b_out[0]

    def initconsts(g):
        g.memset(idb[:, :], 0.0)
        g.memset(mcomb[:, :], 1.0)
        g.memset(ones1f[:, :], 1.0)
        g.memset(ones1b[:, :], 1.0)
        g.memset(zero1[:, :], 0.0)
        return g.memset(eps5[:, :], EPS)

    E("pool", initconsts,
      writes=[b_idb, b_mask, b_ones1f, b_ones1b, b_zero1, b_eps5])
    E("pool", lambda g: g.affine_select(
        out=idb[:, :], in_=idb[:, :], compare_op=ALU.not_equal,
        fill=1.0, base=0, pattern=[[-1, 128]], channel_multiplier=1),
      writes=[b_idb])
    E("pool", lambda g: g.affine_select(
        out=mcomb[:, 0:128], in_=mcomb[:, 0:128], compare_op=ALU.is_ge,
        fill=0.0, base=0, pattern=[[-1, 128]], channel_multiplier=1),
      writes=[b_mask])
    E("pool", lambda g: g.affine_select(
        out=mcomb[:, 128:256], in_=mcomb[:, 128:256], compare_op=ALU.is_ge,
        fill=0.0, base=0, pattern=[[1, 128]], channel_multiplier=-1),
      writes=[b_mask])

    def initvaug(g):
        last = None
        for slot in range(4):
            base = slot * NH * 65
            for h in range(NH):
                last = g.memset(
                    vaug_ring[:, base + 65 * h + 64:base + 65 * h + 65], 1.0)
        return last

    E("pool", initvaug, writes=[b_va[0], b_va[1], b_va[2], b_va[3]])

    for nm in ("wq", "wk", "wv", "wo", "wf"):
        for c in range(8):
            E("sp", (lambda g, nm=nm, c=c:
                     g.dma_start(out=wstage[:, 0:D],
                                 in_=wdram[nm][128 * c:128 * (c + 1), :])),
              writes=[b_wstage])
            E("dve", (lambda g, nm=nm, c=c:
                      g.tensor_copy(w_bf[nm][:, D * c:D * (c + 1)],
                                    wstage[:, 0:D])),
              reads=[b_wstage], writes=[b_w[nm]])

    E("sp", lambda g: g.dma_start(
        out=wstage[:, 0:256].rearrange("p (c r) -> p c r", c=8),
        in_=vr_d.rearrange("(c p) r -> p c r", p=128)),
      writes=[b_wstage])
    E("dve", lambda g: g.tensor_copy(vr_bf[:, :], wstage[:, 0:256]),
      reads=[b_wstage], writes=[b_vr])

    E("sp", lambda g: g.dma_start(out=wstage[0:32, 0:D], in_=ct_d),
      writes=[b_wstage])
    E("dve", lambda g: g.tensor_copy(ct_bf[:, :], wstage[0:32, 0:D]),
      reads=[b_wstage], writes=[b_ct])

    for nm in ("bq", "bk", "bv", "bo", "bf"):
        E("sp", (lambda g, nm=nm: g.dma_start(out=rowtmp[0:1, 0:D],
                                              in_=rows_d[nm])),
          writes=[b_rowtmp])
        E("dve", (lambda g, nm=nm: g.tensor_copy(row_bf[nm][:, :],
                                                 rowtmp[0:1, 0:D])),
          reads=[b_rowtmp], writes=[b_row[nm]])

    for nm, rep, brep, scale in (("lnqw", lnqw_rep, b_lnrep["lnqw"], 1.0),
                                 ("lnqb", lnqb_rep, b_lnrep["lnqb"], 1.0),
                                 ("lnkw", lnkw_rep, b_lnrep["lnkw"], 0.125)):
        E("sp", (lambda g, nm=nm: g.dma_start(out=rowtmp[0:1, 0:D],
                                              in_=rows_d[nm])),
          writes=[b_rowtmp])
        for half in range(2):
            E("pe", (lambda g, half=half:
                     g.matmul(Pout[:, 0:512], ones1f[0:1, :],
                              rowtmp[0:1, 512 * half:512 * (half + 1)],
                              start=True, stop=True)),
              reads=[b_ones1f, b_rowtmp], writes=[b_bank[7]])
            E("act", (lambda g, rep=rep, half=half, scale=scale:
                      g.mul(rep[:, 512 * half:512 * (half + 1)],
                            Pout[:, 0:512], scale)),
              reads=[b_bank[7]], writes=[brep])
        if nm == "lnqw":
            E("dve", lambda g: g.tensor_reduce(
                smallB[0:1, 0:1], rowtmp[0:1, 0:64], AX.X, ALU.max,
                apply_absolute_value=True),
              reads=[b_rowtmp], writes=[b_smallB])
        elif nm == "lnkw":
            E("dve", lambda g: g.tensor_reduce(
                smallB[0:1, 1:2], rowtmp[0:1, 0:64], AX.X, ALU.max,
                apply_absolute_value=True),
              reads=[b_rowtmp], writes=[b_smallB])
        else:
            E("dve", lambda g: g.tensor_tensor(sq64[0:1, :], rowtmp[0:1, 0:64],
                                               rowtmp[0:1, 0:64], ALU.mult),
              reads=[b_rowtmp], writes=[b_sq64])
            E("dve", lambda g: g.tensor_reduce(smallB[0:1, 2:3], sq64[0:1, :],
                                               AX.X, ALU.add),
              reads=[b_sq64], writes=[b_smallB])

    E("pool", lambda g: g.memset(smallB[0:1, 7:8], 1e-30), writes=[b_smallB])
    E("dve", lambda g: g.tensor_tensor(smallB[0:1, 3:4], smallB[0:1, 2:3],
                                       smallB[0:1, 7:8], ALU.add),
      reads=[b_smallB], writes=[b_smallB])
    E("act", lambda g: g.activation(smallB[0:1, 4:5], smallB[0:1, 3:4], AF.Ln,
                                    bias=zero1[0:1, :], scale=1.0),
      reads=[b_smallB, b_zero1], writes=[b_smallB])
    E("act", lambda g: g.activation(smallB[0:1, 4:5], smallB[0:1, 4:5], AF.Exp,
                                    bias=zero1[0:1, :], scale=0.5),
      reads=[b_smallB, b_zero1], writes=[b_smallB])
    E("dve", lambda g: g.tensor_scalar(smallB[0:1, 5:6], smallB[0:1, 0:1],
                                       8.0, None, ALU.mult),
      reads=[b_smallB], writes=[b_smallB])
    E("dve", lambda g: g.tensor_tensor(smallB[0:1, 5:6], smallB[0:1, 5:6],
                                       smallB[0:1, 4:5], ALU.add),
      reads=[b_smallB], writes=[b_smallB])
    E("dve", lambda g: g.tensor_tensor(smallB[0:1, 5:6], smallB[0:1, 5:6],
                                       smallB[0:1, 1:2], ALU.mult),
      reads=[b_smallB], writes=[b_smallB])
    E("dve", lambda g: g.tensor_scalar(smallB[0:1, 6:7], smallB[0:1, 5:6],
                                       -80.0, 0.0, ALU.add, ALU.max),
      reads=[b_smallB], writes=[b_smallB])
    E("pe", lambda g: g.matmul(Pout[:, 0:1], ones1f[0:1, :], smallB[0:1, 6:7],
                               start=True, stop=True),
      reads=[b_ones1f, b_smallB], writes=[b_bank[7]])
    E("dve", lambda g: g.tensor_scalar(negB[:, :], Pout[:, 0:1], -1.0, None,
                                       ALU.mult),
      reads=[b_bank[7]], writes=[b_negB])

    tr_state = [0]

    def transpose8(src, b_src, src_off, dstT, b_dstT, dst_off, evac_eng="dve"):
        par = tr_state[0] % 2
        tr_state[0] += 1
        P = PtrA if par == 0 else PtrB
        pb = b_bank[5 + par]

        def trs(g):
            last = None
            for c in range(8):
                last = g.transpose(
                    P[:, 128 * c:128 * (c + 1)],
                    src[:, src_off + 128 * c:src_off + 128 * (c + 1)],
                    idb[:, :])
            return last
        E("pe", trs, reads=[b_src, b_idb], writes=[pb])
        if evac_eng == "dve":
            E("dve", (lambda g: g.tensor_copy(dstT[:, dst_off:dst_off + D],
                                              P[:, :])),
              reads=[pb], writes=[b_dstT])
        else:
            E("act", (lambda g: g.copy(dstT[:, dst_off:dst_off + D], P[:, :])),
              reads=[pb], writes=[b_dstT])

    def ln_pipeline(src, b_src, src_off, dst, b_dst, dst_off, st_off, sl,
                    bstl, part=None):
        if part != 1:
            E("dve", (lambda g: g.tensor_scalar(
                dst[:, dst_off:dst_off + D], src[:, src_off:src_off + D], 1.0,
                0.0, ALU.mult, ALU.add,
                accum_out=stl[:, st_off:st_off + 1])),
              reads=[b_src], writes=[bstl, b_dst])
        if part == 0:
            return
        E("act", (lambda g: g.activation(
            dst[:, dst_off:dst_off + D], src[:, src_off:src_off + D], AF.Square,
            bias=zero1[:, :], scale=1.0,
            accum_out=stl[:, st_off + 1:st_off + 2])),
          reads=[b_src, b_zero1], writes=[bstl, b_dst])
        E("dve", (lambda g: g.tensor_scalar(
            stl[:, st_off + 2:st_off + 3], stl[:, st_off:st_off + 1],
            1.0 / D, None, ALU.mult)), reads=[bstl], writes=[bstl])
        E("dve", (lambda g: g.tensor_tensor(
            stl[:, st_off + 3:st_off + 4], stl[:, st_off + 2:st_off + 3],
            stl[:, st_off + 2:st_off + 3], ALU.mult)),
          reads=[bstl], writes=[bstl])
        E("dve", (lambda g: g.scalar_tensor_tensor(
            stl[:, st_off + 3:st_off + 4], stl[:, st_off + 3:st_off + 4],
            -float(D), stl[:, st_off + 1:st_off + 2], ALU.mult, ALU.add)),
          reads=[bstl], writes=[bstl])
        E("act", (lambda g: g.activation(
            stl[:, st_off + 4:st_off + 5], stl[:, st_off + 3:st_off + 4],
            AF.Ln, bias=eps5[:, :], scale=1.0 / D)),
          reads=[bstl, b_eps5], writes=[bstl])
        E("act", (lambda g: g.activation(
            stl[:, st_off + 4:st_off + 5], stl[:, st_off + 4:st_off + 5],
            AF.Exp, bias=zero1[:, :], scale=-0.5)),
          reads=[bstl, b_zero1], writes=[bstl])
        E("dve", (lambda g: g.tensor_scalar(
            dst[:, dst_off:dst_off + D], src[:, src_off:src_off + D],
            stl[:, st_off + 2:st_off + 3], stl[:, st_off + 4:st_off + 5],
            ALU.subtract, ALU.mult)),
          reads=[b_src, bstl], writes=[b_dst])

    def head_ln(src_t, b_src, so, dst, b_dst, do, st, b_st, sto,
                gain_rep, b_gain, bias_rep, b_bias, stage, b_mysq):
        mysq = sq_buf if b_mysq is b_sq else sq_buf2
        if stage == 0:
            E("act", (lambda g: g.activation(mysq[:, 0:D], src_t[:, so:so + D],
                                             AF.Square, bias=zero1[:, :],
                                             scale=1.0)),
              reads=[b_src, b_zero1], writes=[b_mysq])
            E("dve", (lambda g: g.tensor_reduce(
                st[:, sto:sto + 16],
                src_t[:, so:so + D].rearrange("p (g c) -> p g c", g=16),
                AX.X, ALU.add)),
              reads=[b_src], writes=[b_st])
        elif stage == 1:
            E("dve", (lambda g: g.tensor_reduce(
                st[:, sto + 16:sto + 32],
                mysq[:, 0:D].rearrange("p (g c) -> p g c", g=16),
                AX.X, ALU.add)),
              reads=[b_mysq], writes=[b_st])
            E("dve", (lambda g: g.tensor_scalar(st[:, sto + 32:sto + 48],
                                                st[:, sto:sto + 16], 1.0 / HD,
                                                None, ALU.mult)),
              reads=[b_st], writes=[b_st])
            E("dve", (lambda g: g.tensor_tensor(st[:, sto + 48:sto + 64],
                                                st[:, sto + 32:sto + 48],
                                                st[:, sto + 32:sto + 48],
                                                ALU.mult)),
              reads=[b_st], writes=[b_st])
            E("dve", (lambda g: g.scalar_tensor_tensor(
                st[:, sto + 48:sto + 64], st[:, sto + 48:sto + 64], -float(HD),
                st[:, sto + 16:sto + 32], ALU.mult, ALU.add)),
              reads=[b_st], writes=[b_st])
        elif stage == 2:
            E("act", (lambda g: g.activation(st[:, sto + 48:sto + 64],
                                             st[:, sto + 48:sto + 64], AF.Ln,
                                             bias=eps5[:, :], scale=1.0 / HD)),
              reads=[b_st, b_eps5], writes=[b_st])
            E("act", (lambda g: g.activation(st[:, sto + 48:sto + 64],
                                             st[:, sto + 48:sto + 64], AF.Exp,
                                             bias=zero1[:, :], scale=-0.5)),
              reads=[b_st, b_zero1], writes=[b_st])
        elif stage == 3:
            def hseg(g):
                last = None
                for h in range(NH):
                    last = g.tensor_scalar(
                        src_t[:, so + HD * h:so + HD * (h + 1)],
                        src_t[:, so + HD * h:so + HD * (h + 1)],
                        st[:, sto + 32 + h:sto + 33 + h],
                        st[:, sto + 48 + h:sto + 49 + h],
                        ALU.subtract, ALU.mult)
                return last
            E("dve", hseg, reads=[b_st], writes=[b_src])
        elif stage == 4:
            if bias_rep is not None:
                E("pool", (lambda g: g.tensor_tensor(src_t[:, so:so + D],
                                                     src_t[:, so:so + D],
                                                     gain_rep[:, :], ALU.mult)),
                  reads=[b_gain], writes=[b_src])
                E("pool", (lambda g: g.tensor_tensor(dst[:, do:do + D],
                                                     src_t[:, so:so + D],
                                                     bias_rep[:, :], ALU.add)),
                  reads=[b_src, b_bias], writes=[b_dst])
            else:
                E("pool", (lambda g: g.tensor_tensor(dst[:, do:do + D],
                                                     src_t[:, so:so + D],
                                                     gain_rep[:, :], ALU.mult)),
                  reads=[b_src, b_gain], writes=[b_dst])

    def blocks_of(t):
        if t == 0:
            return [(0, "c")]
        if t == 1:
            return [(0, None), (1, "c")]
        return [(t - 2, "w"), (t - 1, None), (t, "c")]

    def pacol(h):
        if h < 14:
            return 65 * (h % 7)
        return 65 * (h - 14)

    def pabank(h):
        return 3 if (h < 7 or h >= 14) else 4

    def S0(t):
        r3 = t % 5
        E("sp", (lambda g, t=t, r3=r3: g.dma_start(
            out=x_ring[:, D * r3:D * (r3 + 1)], in_=xd[T * t:T * (t + 1), :])),
          writes=[b_xr[r3]])

    def S1a(t, part=None):
        sl = t % 2
        r3 = t % 5
        ln_pipeline(x_ring, b_xr[r3], D * r3, xn_bf, b_xnb[sl], sl * D,
                    16 * sl, sl, b_stl[sl], part=part)

    def S1b(t):
        sl = t % 2
        transpose8(xn_bf, b_xnb[sl], sl * D, xnT, b_xnT[sl], sl * D, "act")

    def S2h(t, half):
        sl = t % 2
        r4 = t % 4
        if True:
            o0 = 512 * half

            def qkvmm(g, sl=sl, o0=o0):
                last = None
                for ti, (wname, rowname) in enumerate(
                        (("wq", "bq"), ("wk", "bk"), ("wv", "bv"))):
                    pz = Pqkv[:, 512 * ti:512 * (ti + 1)]
                    g.matmul(pz, ones1b[0:1, :], row_bf[rowname][:, o0:o0 + 512],
                             start=True, stop=False)
                    for c in range(8):
                        last = g.matmul(
                            pz, xnT[:, sl * D + 128 * c:sl * D + 128 * (c + 1)],
                            w_bf[wname][:, D * c + o0:D * c + o0 + 512],
                            start=False, stop=(c == 7))
                return last
            E("pe", qkvmm,
              reads=[b_xnT[sl], b_w["wq"], b_w["wk"], b_w["wv"], b_row["bq"],
                     b_row["bk"], b_row["bv"], b_ones1b],
              writes=[b_bank[0], b_bank[1], b_bank[2]])

            E("act", (lambda g, sl=sl, o0=o0: g.copy(
                q_tm[:, sl * D + o0:sl * D + o0 + 512], Pqkv[:, 0:512])),
              reads=[b_bank[0]], writes=[b_qtm[sl]])
            E("act", (lambda g, sl=sl, o0=o0: g.copy(
                k_tm[:, sl * D + o0:sl * D + o0 + 512], Pqkv[:, 512:1024])),
              reads=[b_bank[1]], writes=[b_ktm[sl]])

            def vev(g, r4=r4, half=half):
                base = r4 * NH * 65 + 65 * 8 * half
                outap = bass.AP(vaug_ring, base,
                                [[4 * NH * 65, 128], [65, 8], [1, 64]])
                return g.tensor_copy(
                    outap, Pqkv[:, 1024:1536].rearrange("p (g c) -> p g c", g=8))
            E("dve", vev, reads=[b_bank[2]], writes=[b_va[r4]])

    def S3a(t):
        sl = t % 2
        for stage in range(5):
            head_ln(q_tm, b_qtm[sl], sl * D, qh_bf, b_qhb[sl], sl * D,
                    stq, b_stq[sl], sl * 64,
                    lnqw_rep, b_lnrep["lnqw"], lnqb_rep, b_lnrep["lnqb"],
                    stage, b_sq)
            head_ln(k_tm, b_ktm[sl], sl * D, kh_bf, b_khb[sl], sl * D,
                    stk, b_stk[sl], sl * 64,
                    lnkw_rep, b_lnrep["lnkw"], None, None,
                    stage, b_sq2)

    def S3b(t):
        sl = t % 2
        r4 = t % 4
        transpose8(qh_bf, b_qhb[sl], sl * D, qT, b_qT[sl], sl * D, "act")
        transpose8(kh_bf, b_khb[sl], sl * D, kT_ring, b_kT[r4], r4 * D, "act")

    def S4(t):
        sl = t % 2
        blocks = blocks_of(t)
        nb = len(blocks)

        def emit_qkt(h):
            hp = 64 * (h % 2)
            hc = 128 * (h // 2)
            scs = h % 3

            def qkt(g, sl=sl, hp=hp, hc=hc, scs=scs, blocks=blocks):
                last = None
                for ci, (tk, _m) in enumerate(blocks):
                    last = g.matmul(
                        Pqkv[:, 512 * scs + 128 * ci:512 * scs + 128 * (ci + 1)],
                        kT_ring[hp:hp + 64,
                                (tk % 4) * D + hc:(tk % 4) * D + hc + 128],
                        qT[hp:hp + 64, sl * D + hc:sl * D + hc + 128],
                        start=(ci == 0), stop=(ci == len(blocks) - 1),
                        tile_position=(hp, 0))
                return last
            kreads = [b_kT[tk % 4] for tk, _m in blocks]
            E("pe", qkt, reads=[b_qT[sl]] + kreads, writes=[b_bank[scs]])

        def emit_expmask(p):
            psl = p % 2
            scs0 = (2 * p) % 3
            scs1 = (2 * p + 1) % 3
            step = 512 * (scs1 - scs0)
            po = 768 * psl
            if step > 0:
                def pexp(g, scs0=scs0, step=step, po=po, nb=nb):
                    iap = bass.AP(Pqkv, 512 * scs0,
                                  [[1536, 128], [step, 2], [1, 128 * nb]])
                    oap = bass.AP(probsT, po,
                                  [[1536, 128], [384, 2], [1, 128 * nb]])
                    return g.activation(oap, iap, AF.Exp, bias=negB[:, :],
                                        scale=1.0)
                E("act", pexp, reads=[b_bank[scs0], b_bank[scs1], b_negB],
                  writes=[b_pT[psl]])
            else:
                for i, scs in enumerate((scs0, scs1)):
                    E("act", (lambda g, scs=scs, po=po + 384 * i, nb=nb:
                              g.activation(probsT[:, po:po + 128 * nb],
                                           Pqkv[:, 512 * scs:512 * scs + 128 * nb],
                                           AF.Exp, bias=negB[:, :], scale=1.0)),
                      reads=[b_bank[scs], b_negB], writes=[b_pT[psl]])
            masked = [(ci, m) for ci, (tk, m) in enumerate(blocks) if m]
            for i in range(2):
                meng = "dve"
                hpo = po + 384 * i
                if len(masked) == 2:
                    def hmask(g, hpo=hpo, mlo=masked[0][0]):
                        off = hpo + 128 * mlo
                        io = bass.AP(probsT, off,
                                     [[1536, 128], [256, 2], [1, 128]])
                        mo = bass.AP(mcomb, 0, [[256, 128], [128, 2], [1, 128]])
                        return g.tensor_tensor(io, io, mo, ALU.mult)
                    E(meng, hmask, reads=[b_mask], writes=[b_pT[psl]])
                else:
                    for ci, m in masked:
                        moff = 0 if m == "w" else 128
                        E(meng, (lambda g, hpo=hpo, ci=ci, moff=moff:
                                 g.tensor_tensor(
                                     probsT[:, hpo + 128 * ci:hpo + 128 * (ci + 1)],
                                     probsT[:, hpo + 128 * ci:hpo + 128 * (ci + 1)],
                                     mcomb[:, moff:moff + 128], ALU.mult)),
                          reads=[b_mask], writes=[b_pT[psl]])

        def emit_pv(h):
            psl = (h // 2) % 2
            po = 768 * psl + 384 * (h % 2)

            def pv(g, h=h, po=po, blocks=blocks):
                pc = pacol(h)
                Pz = Pa if pabank(h) == 3 else Pw
                last = None
                for ci, (tk, _m) in enumerate(blocks):
                    last = g.matmul(
                        Pz[:, pc:pc + 65],
                        probsT[:, po + 128 * ci:po + 128 * (ci + 1)],
                        vaug_ring[:, (tk % 4) * NH * 65 + 65 * h:
                                  (tk % 4) * NH * 65 + 65 * (h + 1)],
                        start=(ci == 0 and h in (0, 7, 14)),
                        stop=(ci == len(blocks) - 1 and h in (6, 13, 15)))
                return last
            vreads = [b_va[tk % 4] for tk, _m in blocks]
            E("pe", pv, reads=[b_pT[psl]] + vreads, writes=[b_bank[pabank(h)]])

            if h in (6, 13, 15):
                lo = {6: 0, 13: 7, 15: 14}[h]
                n = h - lo + 1

                Pz2 = Pa if pabank(h) == 3 else Pw
                E("dve", (lambda g, lo=lo, n=n, sl=sl, Pz2=Pz2: g.tensor_scalar(
                    rqt[:, 16 * sl + lo:16 * sl + lo + n],
                    bass.AP(Pz2, pacol(lo) + 64, [[512, 128], [65, n]]),
                    1e-38, None, ALU.add)),
                  reads=[b_bank[pabank(h)]], writes=[b_rqt[sl]])
                E("dve", (lambda g, lo=lo, n=n, sl=sl: g.reciprocal(
                    rq[:, 16 * sl + lo:16 * sl + lo + n],
                    rqt[:, 16 * sl + lo:16 * sl + lo + n])),
                  reads=[b_rqt[sl]], writes=[b_rq[sl]])

                def gevac(g, lo=lo, n=n, sl=sl, Pz2=Pz2):
                    last = None
                    for hh in range(lo, lo + n):
                        last = g.tensor_scalar(
                            att_bf[:, sl * D + HD * hh:sl * D + HD * (hh + 1)],
                            Pz2[:, pacol(hh):pacol(hh) + 64],
                            rq[:, 16 * sl + hh:16 * sl + hh + 1], None,
                            ALU.mult)
                    return last
                E("dve", gevac, reads=[b_bank[pabank(h)], b_rq[sl]],
                  writes=[b_atb[sl]])

        for p in range(8):
            emit_qkt(2 * p)
            emit_qkt(2 * p + 1)
            emit_expmask(p)
            if p >= 1:
                emit_pv(2 * p - 2)
                emit_pv(2 * p - 1)
        emit_pv(14)
        emit_pv(15)

    def S5(t):
        sl = t % 2
        r3 = t % 5
        transpose8(att_bf, b_atb[sl], sl * D, attT, b_attT[sl], sl * D, "act")
        for half in range(2):
            o0 = 512 * half
            bank = 4 if half == 0 else 7
            pz = Pw[:, :] if half == 0 else Pout[:, :]

            def womm(g, sl=sl, o0=o0, pz=pz):
                g.matmul(pz, ones1b[0:1, :], row_bf["bo"][:, o0:o0 + 512],
                         start=True, stop=False)
                last = None
                for c in range(8):
                    last = g.matmul(
                        pz, attT[:, sl * D + 128 * c:sl * D + 128 * (c + 1)],
                        w_bf["wo"][:, D * c + o0:D * c + o0 + 512],
                        start=False, stop=(c == 7))
                return last
            E("pe", womm, reads=[b_attT[sl], b_w["wo"], b_row["bo"], b_ones1b],
              writes=[b_bank[bank]])
            E("dve", (lambda g, sl=sl, o0=o0, r3=r3, pz=pz: g.tensor_tensor(
                x2[:, sl * D + o0:sl * D + o0 + 512], pz,
                x_ring[:, D * r3 + o0:D * r3 + o0 + 512], ALU.add)),
              reads=[b_bank[bank], b_xr[r3]], writes=[b_x2[sl]])

    def S6a(t):
        sl = t % 2
        ln_pipeline(x2, b_x2[sl], sl * D, xn2_bf, b_xn2b[sl], sl * D,
                    16 * sl + 8, sl, b_stl2[sl])

    def S6b1(t):
        sl = t % 2
        transpose8(xn2_bf, b_xn2b[sl], sl * D, xn2T, b_xn2T[sl], sl * D, "act")

        def t1mm(g, sl=sl):
            last = None
            for c in range(8):
                last = g.matmul(Pout[0:32, 0:128], vr_bf[:, 32 * c:32 * (c + 1)],
                                xn2T[:, sl * D + 128 * c:sl * D + 128 * (c + 1)],
                                start=(c == 0), stop=(c == 7))
            return last
        E("pe", t1mm, reads=[b_xn2T[sl], b_vr], writes=[b_bank[7]])
        E("act", (lambda g, sl=sl: g.copy(t1b[:, 128 * sl:128 * (sl + 1)],
                                          Pout[0:32, 0:128])),
          reads=[b_bank[7]], writes=[b_t1b[sl]])

    def S6b2(t):
        sl = t % 2
        for half in range(2):
            o0 = 512 * half
            bank = 4 if half == 0 else 7
            pz = Pw[:, :] if half == 0 else Pout[:, :]

            def ffnmm(g, sl=sl, o0=o0, pz=pz):
                g.matmul(pz, ones1b[0:1, :], row_bf["bf"][:, o0:o0 + 512],
                         start=True, stop=False)
                for c in range(8):
                    g.matmul(pz, xn2T[:, sl * D + 128 * c:sl * D + 128 * (c + 1)],
                             w_bf["wf"][:, D * c + o0:D * c + o0 + 512],
                             start=False, stop=False)
                return g.matmul(pz, t1b[:, 128 * sl:128 * (sl + 1)],
                                ct_bf[:, o0:o0 + 512], start=False, stop=True)
            E("pe", ffnmm, reads=[b_xn2T[sl], b_w["wf"], b_t1b[sl], b_ct,
                                  b_row["bf"], b_ones1b],
              writes=[b_bank[bank]])
            E("dve", (lambda g, sl=sl, o0=o0, pz=pz: g.tensor_tensor(
                out_b[:, sl * D + o0:sl * D + o0 + 512], pz,
                x2[:, sl * D + o0:sl * D + o0 + 512], ALU.add)),
              reads=[b_bank[bank], b_x2[sl]], writes=[b_out[sl]])

        E("sp", (lambda g, t=t, sl=sl: g.dma_start(
            out=outd[T * t:T * (t + 1), :], in_=out_b[:, sl * D:sl * D + D])),
          reads=[b_out[sl]])

    S0(0)
    S0(1)
    S1a(0)
    S1b(0)
    for t in range(NT + 3):
        if t + 1 < NT:
            S1a(t + 1, part=0)
        if 0 <= t - 2 < NT:
            S4(t - 2)
        if t + 1 < NT:
            S1a(t + 1, part=1)
        if 0 <= t - 1 < NT:
            S3a(t - 1)
        if t - 3 >= 0:
            S5(t - 3)
        if t + 2 < NT:
            S0(t + 2)
        if t - 3 >= 0:
            S6a(t - 3)
        if t < NT:
            S2h(t, 0)
        if 0 <= t - 1 < NT:
            S3b(t - 1)
        if t - 3 >= 0:
            S6b1(t - 3)
        if t < NT:
            S2h(t, 1)
        if t - 3 >= 0:
            S6b2(t - 3)
        if t + 1 < NT:
            S1b(t + 1)

    total_sp = s.cnt["sp"]
    E("sp", lambda g: g.nop(), deps=[("sp", total_sp)])

    with (
        nc.semaphore("sem_pe") as sem_pe,
        nc.semaphore("sem_dve") as sem_dve,
        nc.semaphore("sem_act") as sem_act,
        nc.semaphore("sem_pool") as sem_pool,
        nc.semaphore("sem_sp") as sem_sp,
        nc.Block() as block,
    ):
        sems = {"pe": sem_pe, "dve": sem_dve, "act": sem_act,
                "pool": sem_pool, "sp": sem_sp}

        def make_body(name):
            def body(eng):
                for waits, fn in s.streams[name]:
                    for (pn, v) in waits:
                        eng.wait_ge(sems[pn], v)
                    fn(eng).then_inc(sems[name], s.step[name])
            return body

        block.tensor(make_body("pe"))
        block.vector(make_body("dve"))
        block.scalar(make_body("act"))
        block.gpsimd(make_body("pool"))
        block.sync(make_body("sp"))

    return nc


_NC_CACHE = None


def kernel(L0_attn_x, L0_attn_Wq, L0_attn_bq, L0_attn_Wk, L0_attn_bk,
           L0_attn_Wv, L0_attn_bv, L0_attn_Wo, L0_attn_bo,
           L0_attn_ln_q_w, L0_attn_ln_q_b, L0_attn_ln_k_w, L0_attn_ln_k_b,
           L0_ffn_W, L0_ffn_b, L0_ffn_V_r, L0_ffn_C):
    global _NC_CACHE
    x = np.asarray(L0_attn_x, dtype=np.float32)
    assert x.shape == (8, S, D)

    def t32(a):
        return np.ascontiguousarray(np.asarray(a, dtype=np.float32))

    common = {
        "wq_t": t32(np.asarray(L0_attn_Wq).T),
        "wk_t": t32(np.asarray(L0_attn_Wk).T),
        "wv_t": t32(np.asarray(L0_attn_Wv).T),
        "wo_t": t32(np.asarray(L0_attn_Wo).T),
        "wf_t": t32(np.asarray(L0_ffn_W).T),
        "vr": t32(L0_ffn_V_r),
        "ct": t32(np.asarray(L0_ffn_C).T),
        "r_bq": t32(L0_attn_bq)[None, :],
        "r_bk": t32(L0_attn_bk)[None, :],
        "r_bv": t32(L0_attn_bv)[None, :],
        "r_bo": t32(L0_attn_bo)[None, :],
        "r_bf": t32(L0_ffn_b)[None, :],
        "r_lnqw": np.ascontiguousarray(np.tile(t32(L0_attn_ln_q_w), NH))[None, :],
        "r_lnqb": np.ascontiguousarray(np.tile(t32(L0_attn_ln_q_b), NH))[None, :],
        "r_lnkw": np.ascontiguousarray(np.tile(t32(L0_attn_ln_k_w), NH))[None, :],
    }
    in_maps = []
    for b in range(8):
        m = dict(common)
        m["x"] = np.ascontiguousarray(x[b])
        in_maps.append(m)

    if _NC_CACHE is None:
        _NC_CACHE = build_nc()
    res = run_bass_kernel_spmd(_NC_CACHE, in_maps, core_ids=list(range(8)))
    out = np.stack([np.asarray(r["out"], dtype=np.float32) for r in res.results])
    return out


# version 35
# speedup vs baseline: 1.1888x; 1.1888x over previous
import os
import numpy as np
import concourse.bass as bass
import concourse.mybir as mybir
from concourse.bass_utils import run_bass_kernel_spmd

F32 = mybir.dt.float32
BF16 = mybir.dt.bfloat16
AF = mybir.ActivationFunctionType
ALU = mybir.AluOpType
AX = mybir.AxisListType

S = 4096
D = 1024
T = 128
NT = S // T
NH = 16
HD = 64
EPS = 1e-5

ENGINES = ("pe", "dve", "act", "pool", "sp")
SELF_WAIT = {"dve", "act", "pool", "sp"}


class Buf:
    __slots__ = ("name", "writers", "readers")

    def __init__(self, name):
        self.name = name
        self.writers = {}
        self.readers = {}


class Sched:
    def __init__(self):
        self.streams = {e: [] for e in ENGINES}
        self.cnt = {e: 0 for e in ENGINES}
        self.step = {e: (16 if e == "sp" else 1) for e in ENGINES}
        self.lastw = {}

    def emit(self, eng, fn, reads=(), writes=(), deps=()):
        need = {}

        def add(e, c):
            need[e] = max(need.get(e, 0), c)

        for b in reads:
            for e, c in b.writers.items():
                add(e, c)
        for b in writes:
            for e, c in b.writers.items():
                add(e, c)
            for e, c in b.readers.items():
                add(e, c)
        for e, c in deps:
            add(e, c)
        waits = []
        for e, c in sorted(need.items()):
            if e == eng and eng not in SELF_WAIT:
                continue
            k = (eng, e)
            if self.lastw.get(k, 0) >= c:
                continue
            self.lastw[k] = c
            waits.append((e, c * self.step[e]))
        self.cnt[eng] += 1
        me = self.cnt[eng]
        self.streams[eng].append((waits, fn))
        for b in reads:
            b.readers[eng] = max(b.readers.get(eng, 0), me)
        for b in writes:
            b.writers[eng] = max(b.writers.get(eng, 0), me)
        return (eng, me)


DBG = int(os.environ.get("KDBG", "0"))
DT = 2


def build_nc():
    nc = bass.Bass("TRN2", target_bir_lowering=False, dynamic_dma_scratch_size=4096)
    dbg = {}

    def dbg_out(name, shape, dtype):
        dbg[name] = nc.dram_tensor("dbg_" + name, shape, dtype,
                                   kind="ExternalOutput").ap()
        return dbg[name]

    xd = nc.dram_tensor("x", [S, D], F32, kind="ExternalInput").ap()
    outd = nc.dram_tensor("out", [S, D], F32, kind="ExternalOutput").ap()
    wdram = {}
    for nm in ("wq", "wk", "wv", "wo", "wf"):
        wdram[nm] = nc.dram_tensor(nm + "_t", [D, D], F32, kind="ExternalInput").ap()
    vr_d = nc.dram_tensor("vr", [D, 32], F32, kind="ExternalInput").ap()
    ct_d = nc.dram_tensor("ct", [32, D], F32, kind="ExternalInput").ap()
    rows_d = {}
    for nm in ("bq", "bk", "bv", "bo", "bf", "lnqw", "lnqb", "lnkw"):
        rows_d[nm] = nc.dram_tensor("r_" + nm, [1, D], F32, kind="ExternalInput").ap()

    s = Sched()
    E = s.emit

    def sb(name, shape, dtype):
        return nc.alloc_sbuf_tensor(name, shape, dtype)

    w_bf = {nm: sb(nm + "_bf", [128, 8 * D], BF16)
            for nm in ("wq", "wk", "wv", "wo", "wf")}
    vr_bf = sb("vr_bf", [128, 8 * 32], BF16)
    ct_bf = sb("ct_bf", [32, D], BF16)
    row_bf = {nm: sb("row_" + nm, [1, D], BF16)
              for nm in ("bq", "bk", "bv", "bo", "bf")}
    lnqw_rep = sb("lnqw_rep", [128, D], F32)
    lnqb_rep = sb("lnqb_rep", [128, D], F32)
    lnkw_rep = sb("lnkw_rep", [128, D], F32)
    idb = sb("idb", [128, 128], BF16)
    mcomb = sb("mcomb", [128, 256], BF16)
    ones1f = sb("ones1f", [1, 128], F32)
    ones1b = sb("ones1b", [1, 128], BF16)
    negB = sb("negB", [128, 1], F32)
    zero1 = sb("zero1", [128, 1], F32)
    eps5 = sb("eps5", [128, 1], F32)
    smallB = sb("smallB", [1, 8], F32)
    sq64 = sb("sq64", [1, 64], F32)

    x_ring = sb("x_ring", [128, 5 * D], F32)
    kT_ring = sb("kT_ring", [128, 4 * D], BF16)
    vaug_ring = sb("vaug_ring", [128, 4 * NH * 65], BF16)
    qT = sb("qT", [128, 2 * D], BF16)
    xnT = sb("xnT", [128, 2 * D], BF16)
    xn2T = sb("xn2T", [128, 2 * D], BF16)
    attT = sb("attT", [128, 2 * D], BF16)
    q_tm = sb("q_tm", [128, 2 * D], F32)
    k_tm = sb("k_tm", [128, 2 * D], F32)
    qh_bf = sb("qh_bf", [128, 2 * D], BF16)
    kh_bf = sb("kh_bf", [128, 2 * D], BF16)
    xn_bf = sb("xn_bf", [128, 2 * D], BF16)
    xn2_bf = sb("xn2_bf", [128, 2 * D], BF16)
    att_bf = sb("att_bf", [128, 2 * D], BF16)
    x2 = sb("x2", [128, 2 * D], F32)
    out_b = x2
    sq_buf = sb("sq_buf", [128, D], F32)
    sq_buf2 = sb("sq_buf2", [128, D], F32)
    probsT = sb("probsT", [128, 2 * 768], BF16)
    t1b = sb("t1b", [32, 2 * 128], BF16)
    stq = sb("stq", [128, 2 * 64], F32)
    stk = sb("stk", [128, 2 * 64], F32)
    stl = sb("stl", [128, 2 * 16], F32)
    rq = sb("rq", [128, 2 * 16], F32)
    rqt = sb("rqt", [128, 2 * 16], F32)

    wstage = x_ring
    rowtmp = out_b

    Pqkv = nc.alloc_psum_tensor("Pqkv", [128, 1536], F32)
    Pa = nc.alloc_psum_tensor("Pa", [128, 512], F32)
    Pw = nc.alloc_psum_tensor("Pw", [128, 512], F32)
    PtrA = nc.alloc_psum_tensor("PtrA", [128, 1024], BF16)
    PtrB = nc.alloc_psum_tensor("PtrB", [128, 1024], BF16)
    Pout = nc.alloc_psum_tensor("Pout", [128, 512], F32)

    b_w = {nm: Buf("w_" + nm) for nm in w_bf}
    b_vr, b_ct = Buf("vr"), Buf("ct")
    b_row = {nm: Buf("row_" + nm) for nm in row_bf}
    b_lnrep = {"lnqw": Buf("lnqw_rep"), "lnqb": Buf("lnqb_rep"),
               "lnkw": Buf("lnkw_rep")}
    b_idb, b_mask = Buf("idb"), Buf("mask")
    b_ones1f, b_ones1b = Buf("ones1f"), Buf("ones1b")
    b_negB, b_zero1, b_eps5 = Buf("negB"), Buf("zero1"), Buf("eps5")
    b_smallB, b_sq64 = Buf("smallB"), Buf("sq64")
    b_xr = [Buf(f"x{i}") for i in range(5)]
    b_kT = [Buf(f"kT{i}") for i in range(4)]
    b_va = [Buf(f"va{i}") for i in range(4)]
    b_qT = [Buf(f"qT{i}") for i in range(2)]
    b_xnT = [Buf(f"xnT{i}") for i in range(2)]
    b_xn2T = [Buf(f"xn2T{i}") for i in range(2)]
    b_attT = [Buf(f"attT{i}") for i in range(2)]
    b_qtm = [Buf(f"qtm{i}") for i in range(2)]
    b_ktm = [Buf(f"ktm{i}") for i in range(2)]
    b_qhb = [Buf(f"qhb{i}") for i in range(2)]
    b_khb = [Buf(f"khb{i}") for i in range(2)]
    b_xnb = [Buf(f"xnb{i}") for i in range(2)]
    b_xn2b = [Buf(f"xn2b{i}") for i in range(2)]
    b_atb = [Buf(f"atb{i}") for i in range(2)]
    b_x2 = [Buf(f"x2_{i}") for i in range(2)]
    b_out = b_x2
    b_sq = Buf("sq_buf")
    b_sq2 = Buf("sq_buf2")
    b_pT = [Buf(f"pT{i}") for i in range(2)]
    b_t1b = [Buf(f"t1b{i}") for i in range(2)]
    b_stq = [Buf(f"stq{i}") for i in range(2)]
    b_stk = [Buf(f"stk{i}") for i in range(2)]
    b_stl = [Buf(f"stl{i}") for i in range(2)]
    b_stl2 = [Buf(f"stl2_{i}") for i in range(2)]
    b_rq = [Buf(f"rq{i}") for i in range(2)]
    b_rqt = [Buf(f"rqt{i}") for i in range(2)]
    b_bank = [Buf(f"bank{i}") for i in range(8)]
    b_wstage = b_xr[0]
    b_rowtmp = b_out[0]

    def initconsts(g):
        g.memset(idb[:, :], 0.0)
        g.memset(mcomb[:, :], 1.0)
        g.memset(ones1f[:, :], 1.0)
        g.memset(ones1b[:, :], 1.0)
        g.memset(zero1[:, :], 0.0)
        return g.memset(eps5[:, :], EPS)

    E("pool", initconsts,
      writes=[b_idb, b_mask, b_ones1f, b_ones1b, b_zero1, b_eps5])
    E("pool", lambda g: g.affine_select(
        out=idb[:, :], in_=idb[:, :], compare_op=ALU.not_equal,
        fill=1.0, base=0, pattern=[[-1, 128]], channel_multiplier=1),
      writes=[b_idb])
    E("pool", lambda g: g.affine_select(
        out=mcomb[:, 0:128], in_=mcomb[:, 0:128], compare_op=ALU.is_ge,
        fill=0.0, base=0, pattern=[[-1, 128]], channel_multiplier=1),
      writes=[b_mask])
    E("pool", lambda g: g.affine_select(
        out=mcomb[:, 128:256], in_=mcomb[:, 128:256], compare_op=ALU.is_ge,
        fill=0.0, base=0, pattern=[[1, 128]], channel_multiplier=-1),
      writes=[b_mask])

    def initvaug(g):
        last = None
        for slot in range(4):
            base = slot * NH * 65
            for h in range(NH):
                last = g.memset(
                    vaug_ring[:, base + 65 * h + 64:base + 65 * h + 65], 1.0)
        return last

    E("pool", initvaug, writes=[b_va[0], b_va[1], b_va[2], b_va[3]])

    for nm in ("wq", "wk", "wv", "wo", "wf"):
        for c in range(8):
            E("sp", (lambda g, nm=nm, c=c:
                     g.dma_start(out=wstage[:, 0:D],
                                 in_=wdram[nm][128 * c:128 * (c + 1), :])),
              writes=[b_wstage])
            E("dve", (lambda g, nm=nm, c=c:
                      g.tensor_copy(w_bf[nm][:, D * c:D * (c + 1)],
                                    wstage[:, 0:D])),
              reads=[b_wstage], writes=[b_w[nm]])

    E("sp", lambda g: g.dma_start(
        out=wstage[:, 0:256].rearrange("p (c r) -> p c r", c=8),
        in_=vr_d.rearrange("(c p) r -> p c r", p=128)),
      writes=[b_wstage])
    E("dve", lambda g: g.tensor_copy(vr_bf[:, :], wstage[:, 0:256]),
      reads=[b_wstage], writes=[b_vr])

    E("sp", lambda g: g.dma_start(out=wstage[0:32, 0:D], in_=ct_d),
      writes=[b_wstage])
    E("dve", lambda g: g.tensor_copy(ct_bf[:, :], wstage[0:32, 0:D]),
      reads=[b_wstage], writes=[b_ct])

    for nm in ("bq", "bk", "bv", "bo", "bf"):
        E("sp", (lambda g, nm=nm: g.dma_start(out=rowtmp[0:1, 0:D],
                                              in_=rows_d[nm])),
          writes=[b_rowtmp])
        E("dve", (lambda g, nm=nm: g.tensor_copy(row_bf[nm][:, :],
                                                 rowtmp[0:1, 0:D])),
          reads=[b_rowtmp], writes=[b_row[nm]])

    for nm, rep, brep, scale in (("lnqw", lnqw_rep, b_lnrep["lnqw"], 1.0),
                                 ("lnqb", lnqb_rep, b_lnrep["lnqb"], 1.0),
                                 ("lnkw", lnkw_rep, b_lnrep["lnkw"], 0.125)):
        E("sp", (lambda g, nm=nm: g.dma_start(out=rowtmp[0:1, 0:D],
                                              in_=rows_d[nm])),
          writes=[b_rowtmp])
        for half in range(2):
            E("pe", (lambda g, half=half:
                     g.matmul(Pout[:, 0:512], ones1f[0:1, :],
                              rowtmp[0:1, 512 * half:512 * (half + 1)],
                              start=True, stop=True)),
              reads=[b_ones1f, b_rowtmp], writes=[b_bank[7]])
            E("act", (lambda g, rep=rep, half=half, scale=scale:
                      g.mul(rep[:, 512 * half:512 * (half + 1)],
                            Pout[:, 0:512], scale)),
              reads=[b_bank[7]], writes=[brep])
        if nm == "lnqw":
            E("dve", lambda g: g.tensor_reduce(
                smallB[0:1, 0:1], rowtmp[0:1, 0:64], AX.X, ALU.max,
                apply_absolute_value=True),
              reads=[b_rowtmp], writes=[b_smallB])
        elif nm == "lnkw":
            E("dve", lambda g: g.tensor_reduce(
                smallB[0:1, 1:2], rowtmp[0:1, 0:64], AX.X, ALU.max,
                apply_absolute_value=True),
              reads=[b_rowtmp], writes=[b_smallB])
        else:
            E("dve", lambda g: g.tensor_tensor(sq64[0:1, :], rowtmp[0:1, 0:64],
                                               rowtmp[0:1, 0:64], ALU.mult),
              reads=[b_rowtmp], writes=[b_sq64])
            E("dve", lambda g: g.tensor_reduce(smallB[0:1, 2:3], sq64[0:1, :],
                                               AX.X, ALU.add),
              reads=[b_sq64], writes=[b_smallB])

    E("pool", lambda g: g.memset(smallB[0:1, 7:8], 1e-30), writes=[b_smallB])
    E("dve", lambda g: g.tensor_tensor(smallB[0:1, 3:4], smallB[0:1, 2:3],
                                       smallB[0:1, 7:8], ALU.add),
      reads=[b_smallB], writes=[b_smallB])
    E("act", lambda g: g.activation(smallB[0:1, 4:5], smallB[0:1, 3:4], AF.Ln,
                                    bias=zero1[0:1, :], scale=1.0),
      reads=[b_smallB, b_zero1], writes=[b_smallB])
    E("act", lambda g: g.activation(smallB[0:1, 4:5], smallB[0:1, 4:5], AF.Exp,
                                    bias=zero1[0:1, :], scale=0.5),
      reads=[b_smallB, b_zero1], writes=[b_smallB])
    E("dve", lambda g: g.tensor_scalar(smallB[0:1, 5:6], smallB[0:1, 0:1],
                                       8.0, None, ALU.mult),
      reads=[b_smallB], writes=[b_smallB])
    E("dve", lambda g: g.tensor_tensor(smallB[0:1, 5:6], smallB[0:1, 5:6],
                                       smallB[0:1, 4:5], ALU.add),
      reads=[b_smallB], writes=[b_smallB])
    E("dve", lambda g: g.tensor_tensor(smallB[0:1, 5:6], smallB[0:1, 5:6],
                                       smallB[0:1, 1:2], ALU.mult),
      reads=[b_smallB], writes=[b_smallB])
    E("dve", lambda g: g.tensor_scalar(smallB[0:1, 6:7], smallB[0:1, 5:6],
                                       -80.0, 0.0, ALU.add, ALU.max),
      reads=[b_smallB], writes=[b_smallB])
    E("pe", lambda g: g.matmul(Pout[:, 0:1], ones1f[0:1, :], smallB[0:1, 6:7],
                               start=True, stop=True),
      reads=[b_ones1f, b_smallB], writes=[b_bank[7]])
    E("dve", lambda g: g.tensor_scalar(negB[:, :], Pout[:, 0:1], -1.0, None,
                                       ALU.mult),
      reads=[b_bank[7]], writes=[b_negB])

    tr_state = [0]

    def transpose8(src, b_src, src_off, dstT, b_dstT, dst_off, evac_eng="dve"):
        par = tr_state[0] % 2
        tr_state[0] += 1
        P = PtrA if par == 0 else PtrB
        pb = b_bank[5 + par]

        def trs(g):
            last = None
            for c in range(8):
                last = g.transpose(
                    P[:, 128 * c:128 * (c + 1)],
                    src[:, src_off + 128 * c:src_off + 128 * (c + 1)],
                    idb[:, :])
            return last
        E("pe", trs, reads=[b_src, b_idb], writes=[pb])
        if evac_eng == "dve":
            E("dve", (lambda g: g.tensor_copy(dstT[:, dst_off:dst_off + D],
                                              P[:, :])),
              reads=[pb], writes=[b_dstT])
        else:
            E("act", (lambda g: g.copy(dstT[:, dst_off:dst_off + D], P[:, :])),
              reads=[pb], writes=[b_dstT])

    def ln_pipeline(src, b_src, src_off, dst, b_dst, dst_off, st_off, sl,
                    bstl, part=None):
        if part != 1:
            E("dve", (lambda g: g.tensor_scalar(
                dst[:, dst_off:dst_off + D], src[:, src_off:src_off + D], 1.0,
                0.0, ALU.mult, ALU.add,
                accum_out=stl[:, st_off:st_off + 1])),
              reads=[b_src], writes=[bstl, b_dst])
        if part == 0:
            return
        E("act", (lambda g: g.activation(
            dst[:, dst_off:dst_off + D], src[:, src_off:src_off + D], AF.Square,
            bias=zero1[:, :], scale=1.0,
            accum_out=stl[:, st_off + 1:st_off + 2])),
          reads=[b_src, b_zero1], writes=[bstl, b_dst])
        E("dve", (lambda g: g.tensor_scalar(
            stl[:, st_off + 2:st_off + 3], stl[:, st_off:st_off + 1],
            1.0 / D, None, ALU.mult)), reads=[bstl], writes=[bstl])
        E("dve", (lambda g: g.tensor_tensor(
            stl[:, st_off + 3:st_off + 4], stl[:, st_off + 2:st_off + 3],
            stl[:, st_off + 2:st_off + 3], ALU.mult)),
          reads=[bstl], writes=[bstl])
        E("dve", (lambda g: g.scalar_tensor_tensor(
            stl[:, st_off + 3:st_off + 4], stl[:, st_off + 3:st_off + 4],
            -float(D), stl[:, st_off + 1:st_off + 2], ALU.mult, ALU.add)),
          reads=[bstl], writes=[bstl])
        E("act", (lambda g: g.activation(
            stl[:, st_off + 4:st_off + 5], stl[:, st_off + 3:st_off + 4],
            AF.Ln, bias=eps5[:, :], scale=1.0 / D)),
          reads=[bstl, b_eps5], writes=[bstl])
        E("act", (lambda g: g.activation(
            stl[:, st_off + 4:st_off + 5], stl[:, st_off + 4:st_off + 5],
            AF.Exp, bias=zero1[:, :], scale=-0.5)),
          reads=[bstl, b_zero1], writes=[bstl])
        E("dve", (lambda g: g.tensor_scalar(
            dst[:, dst_off:dst_off + D], src[:, src_off:src_off + D],
            stl[:, st_off + 2:st_off + 3], stl[:, st_off + 4:st_off + 5],
            ALU.subtract, ALU.mult)),
          reads=[b_src, bstl], writes=[b_dst])

    def head_ln(src_t, b_src, so, dst, b_dst, do, st, b_st, sto,
                gain_rep, b_gain, bias_rep, b_bias, stage, b_mysq):
        mysq = sq_buf if b_mysq is b_sq else sq_buf2
        if stage == 0:
            E("act", (lambda g: g.activation(mysq[:, 0:D], src_t[:, so:so + D],
                                             AF.Square, bias=zero1[:, :],
                                             scale=1.0)),
              reads=[b_src, b_zero1], writes=[b_mysq])
            E("dve", (lambda g: g.tensor_reduce(
                st[:, sto:sto + 16],
                src_t[:, so:so + D].rearrange("p (g c) -> p g c", g=16),
                AX.X, ALU.add)),
              reads=[b_src], writes=[b_st])
        elif stage == 1:
            E("dve", (lambda g: g.tensor_reduce(
                st[:, sto + 16:sto + 32],
                mysq[:, 0:D].rearrange("p (g c) -> p g c", g=16),
                AX.X, ALU.add)),
              reads=[b_mysq], writes=[b_st])
            E("dve", (lambda g: g.tensor_scalar(st[:, sto + 32:sto + 48],
                                                st[:, sto:sto + 16], 1.0 / HD,
                                                None, ALU.mult)),
              reads=[b_st], writes=[b_st])
            E("dve", (lambda g: g.tensor_tensor(st[:, sto + 48:sto + 64],
                                                st[:, sto + 32:sto + 48],
                                                st[:, sto + 32:sto + 48],
                                                ALU.mult)),
              reads=[b_st], writes=[b_st])
            E("dve", (lambda g: g.scalar_tensor_tensor(
                st[:, sto + 48:sto + 64], st[:, sto + 48:sto + 64], -float(HD),
                st[:, sto + 16:sto + 32], ALU.mult, ALU.add)),
              reads=[b_st], writes=[b_st])
        elif stage == 2:
            E("act", (lambda g: g.activation(st[:, sto + 48:sto + 64],
                                             st[:, sto + 48:sto + 64], AF.Ln,
                                             bias=eps5[:, :], scale=1.0 / HD)),
              reads=[b_st, b_eps5], writes=[b_st])
            E("act", (lambda g: g.activation(st[:, sto + 48:sto + 64],
                                             st[:, sto + 48:sto + 64], AF.Exp,
                                             bias=zero1[:, :], scale=-0.5)),
              reads=[b_st, b_zero1], writes=[b_st])
        elif stage == 3:
            def hseg(g):
                last = None
                for h in range(NH):
                    last = g.tensor_scalar(
                        src_t[:, so + HD * h:so + HD * (h + 1)],
                        src_t[:, so + HD * h:so + HD * (h + 1)],
                        st[:, sto + 32 + h:sto + 33 + h],
                        st[:, sto + 48 + h:sto + 49 + h],
                        ALU.subtract, ALU.mult)
                return last
            E("dve", hseg, reads=[b_st], writes=[b_src])
        elif stage == 4:
            if bias_rep is not None:
                E("pool", (lambda g: g.tensor_tensor(src_t[:, so:so + D],
                                                     src_t[:, so:so + D],
                                                     gain_rep[:, :], ALU.mult)),
                  reads=[b_gain], writes=[b_src])
                E("pool", (lambda g: g.tensor_tensor(dst[:, do:do + D],
                                                     src_t[:, so:so + D],
                                                     bias_rep[:, :], ALU.add)),
                  reads=[b_src, b_bias], writes=[b_dst])
            else:
                E("pool", (lambda g: g.tensor_tensor(dst[:, do:do + D],
                                                     src_t[:, so:so + D],
                                                     gain_rep[:, :], ALU.mult)),
                  reads=[b_src, b_gain], writes=[b_dst])

    def blocks_of(t):
        if t == 0:
            return [(0, "c")]
        if t == 1:
            return [(0, None), (1, "c")]
        return [(t - 2, "w"), (t - 1, None), (t, "c")]

    def pacol(h):
        if h < 14:
            return 65 * (h % 7)
        return 65 * (h - 14)

    def pabank(h):
        return 3 if (h < 7 or h >= 14) else 4

    def S0(t):
        r3 = t % 5
        E("sp", (lambda g, t=t, r3=r3: g.dma_start(
            out=x_ring[:, D * r3:D * (r3 + 1)], in_=xd[T * t:T * (t + 1), :])),
          writes=[b_xr[r3]])

    def S1a(t, part=None):
        sl = t % 2
        r3 = t % 5
        ln_pipeline(x_ring, b_xr[r3], D * r3, xn_bf, b_xnb[sl], sl * D,
                    16 * sl, sl, b_stl[sl], part=part)

    def S1b(t):
        sl = t % 2
        transpose8(xn_bf, b_xnb[sl], sl * D, xnT, b_xnT[sl], sl * D, "act")

    def S2h(t, half):
        sl = t % 2
        r4 = t % 4
        if True:
            o0 = 512 * half

            def qkvmm(g, sl=sl, o0=o0):
                last = None
                for ti, (wname, rowname) in enumerate(
                        (("wq", "bq"), ("wk", "bk"), ("wv", "bv"))):
                    pz = Pqkv[:, 512 * ti:512 * (ti + 1)]
                    g.matmul(pz, ones1b[0:1, :], row_bf[rowname][:, o0:o0 + 512],
                             start=True, stop=False)
                    for c in range(8):
                        last = g.matmul(
                            pz, xnT[:, sl * D + 128 * c:sl * D + 128 * (c + 1)],
                            w_bf[wname][:, D * c + o0:D * c + o0 + 512],
                            start=False, stop=(c == 7))
                return last
            E("pe", qkvmm,
              reads=[b_xnT[sl], b_w["wq"], b_w["wk"], b_w["wv"], b_row["bq"],
                     b_row["bk"], b_row["bv"], b_ones1b],
              writes=[b_bank[0], b_bank[1], b_bank[2]])

            E("act", (lambda g, sl=sl, o0=o0: g.copy(
                q_tm[:, sl * D + o0:sl * D + o0 + 512], Pqkv[:, 0:512])),
              reads=[b_bank[0]], writes=[b_qtm[sl]])
            E("act", (lambda g, sl=sl, o0=o0: g.copy(
                k_tm[:, sl * D + o0:sl * D + o0 + 512], Pqkv[:, 512:1024])),
              reads=[b_bank[1]], writes=[b_ktm[sl]])

            def vev(g, r4=r4, half=half):
                base = r4 * NH * 65 + 65 * 8 * half
                outap = bass.AP(vaug_ring, base,
                                [[4 * NH * 65, 128], [65, 8], [1, 64]])
                return g.tensor_copy(
                    outap, Pqkv[:, 1024:1536].rearrange("p (g c) -> p g c", g=8))
            E("dve", vev, reads=[b_bank[2]], writes=[b_va[r4]])

    def S3a(t):
        sl = t % 2
        for stage in range(5):
            head_ln(q_tm, b_qtm[sl], sl * D, qh_bf, b_qhb[sl], sl * D,
                    stq, b_stq[sl], sl * 64,
                    lnqw_rep, b_lnrep["lnqw"], lnqb_rep, b_lnrep["lnqb"],
                    stage, b_sq)
            head_ln(k_tm, b_ktm[sl], sl * D, kh_bf, b_khb[sl], sl * D,
                    stk, b_stk[sl], sl * 64,
                    lnkw_rep, b_lnrep["lnkw"], None, None,
                    stage, b_sq2)

    def S3b(t):
        sl = t % 2
        r4 = t % 4
        transpose8(qh_bf, b_qhb[sl], sl * D, qT, b_qT[sl], sl * D, "act")
        transpose8(kh_bf, b_khb[sl], sl * D, kT_ring, b_kT[r4], r4 * D, "act")

    def S4(t):
        sl = t % 2
        blocks = blocks_of(t)
        nb = len(blocks)

        def emit_qkt(h):
            hp = 64 * (h % 2)
            hc = 128 * (h // 2)
            scs = h % 3

            def qkt(g, sl=sl, hp=hp, hc=hc, scs=scs, blocks=blocks):
                last = None
                for ci, (tk, _m) in enumerate(blocks):
                    last = g.matmul(
                        Pqkv[:, 512 * scs + 128 * ci:512 * scs + 128 * (ci + 1)],
                        kT_ring[hp:hp + 64,
                                (tk % 4) * D + hc:(tk % 4) * D + hc + 128],
                        qT[hp:hp + 64, sl * D + hc:sl * D + hc + 128],
                        start=(ci == 0), stop=(ci == len(blocks) - 1),
                        tile_position=(hp, 0))
                return last
            kreads = [b_kT[tk % 4] for tk, _m in blocks]
            E("pe", qkt, reads=[b_qT[sl]] + kreads, writes=[b_bank[scs]])

        def emit_expmask(p):
            psl = p % 2
            scs0 = (2 * p) % 3
            scs1 = (2 * p + 1) % 3
            step = 512 * (scs1 - scs0)
            po = 768 * psl
            if step > 0:
                def pexp(g, scs0=scs0, step=step, po=po, nb=nb):
                    iap = bass.AP(Pqkv, 512 * scs0,
                                  [[1536, 128], [step, 2], [1, 128 * nb]])
                    oap = bass.AP(probsT, po,
                                  [[1536, 128], [384, 2], [1, 128 * nb]])
                    return g.activation(oap, iap, AF.Exp, bias=negB[:, :],
                                        scale=1.0)
                E("act", pexp, reads=[b_bank[scs0], b_bank[scs1], b_negB],
                  writes=[b_pT[psl]])
            else:
                for i, scs in enumerate((scs0, scs1)):
                    E("act", (lambda g, scs=scs, po=po + 384 * i, nb=nb:
                              g.activation(probsT[:, po:po + 128 * nb],
                                           Pqkv[:, 512 * scs:512 * scs + 128 * nb],
                                           AF.Exp, bias=negB[:, :], scale=1.0)),
                      reads=[b_bank[scs], b_negB], writes=[b_pT[psl]])
            masked = [(ci, m) for ci, (tk, m) in enumerate(blocks) if m]
            for i in range(2):
                meng = "dve"
                hpo = po + 384 * i
                if len(masked) == 2:
                    def hmask(g, hpo=hpo, mlo=masked[0][0]):
                        off = hpo + 128 * mlo
                        io = bass.AP(probsT, off,
                                     [[1536, 128], [256, 2], [1, 128]])
                        mo = bass.AP(mcomb, 0, [[256, 128], [128, 2], [1, 128]])
                        return g.tensor_tensor(io, io, mo, ALU.mult)
                    E(meng, hmask, reads=[b_mask], writes=[b_pT[psl]])
                else:
                    for ci, m in masked:
                        moff = 0 if m == "w" else 128
                        E(meng, (lambda g, hpo=hpo, ci=ci, moff=moff:
                                 g.tensor_tensor(
                                     probsT[:, hpo + 128 * ci:hpo + 128 * (ci + 1)],
                                     probsT[:, hpo + 128 * ci:hpo + 128 * (ci + 1)],
                                     mcomb[:, moff:moff + 128], ALU.mult)),
                          reads=[b_mask], writes=[b_pT[psl]])

        def emit_pv(h):
            psl = (h // 2) % 2
            po = 768 * psl + 384 * (h % 2)

            def pv(g, h=h, po=po, blocks=blocks):
                pc = pacol(h)
                Pz = Pa if pabank(h) == 3 else Pw
                last = None
                for ci, (tk, _m) in enumerate(blocks):
                    last = g.matmul(
                        Pz[:, pc:pc + 65],
                        probsT[:, po + 128 * ci:po + 128 * (ci + 1)],
                        vaug_ring[:, (tk % 4) * NH * 65 + 65 * h:
                                  (tk % 4) * NH * 65 + 65 * (h + 1)],
                        start=(ci == 0 and h in (0, 7, 14)),
                        stop=(ci == len(blocks) - 1 and h in (6, 13, 15)))
                return last
            vreads = [b_va[tk % 4] for tk, _m in blocks]
            E("pe", pv, reads=[b_pT[psl]] + vreads, writes=[b_bank[pabank(h)]])

            if h in (6, 13, 15):
                lo = {6: 0, 13: 7, 15: 14}[h]
                n = h - lo + 1

                Pz2 = Pa if pabank(h) == 3 else Pw
                E("dve", (lambda g, lo=lo, n=n, sl=sl, Pz2=Pz2: g.tensor_scalar(
                    rqt[:, 16 * sl + lo:16 * sl + lo + n],
                    bass.AP(Pz2, pacol(lo) + 64, [[512, 128], [65, n]]),
                    1e-38, None, ALU.add)),
                  reads=[b_bank[pabank(h)]], writes=[b_rqt[sl]])
                E("dve", (lambda g, lo=lo, n=n, sl=sl: g.reciprocal(
                    rq[:, 16 * sl + lo:16 * sl + lo + n],
                    rqt[:, 16 * sl + lo:16 * sl + lo + n])),
                  reads=[b_rqt[sl]], writes=[b_rq[sl]])

                def gevac(g, lo=lo, n=n, sl=sl, Pz2=Pz2):
                    last = None
                    for hh in range(lo, lo + n):
                        last = g.tensor_scalar(
                            att_bf[:, sl * D + HD * hh:sl * D + HD * (hh + 1)],
                            Pz2[:, pacol(hh):pacol(hh) + 64],
                            rq[:, 16 * sl + hh:16 * sl + hh + 1], None,
                            ALU.mult)
                    return last
                E("dve", gevac, reads=[b_bank[pabank(h)], b_rq[sl]],
                  writes=[b_atb[sl]])

        for p in range(8):
            emit_qkt(2 * p)
            emit_qkt(2 * p + 1)
            emit_expmask(p)
            if p >= 1:
                emit_pv(2 * p - 2)
                emit_pv(2 * p - 1)
        emit_pv(14)
        emit_pv(15)

    def S5(t):
        sl = t % 2
        r3 = t % 5
        transpose8(att_bf, b_atb[sl], sl * D, attT, b_attT[sl], sl * D, "act")
        for half in range(2):
            o0 = 512 * half
            bank = 4 if half == 0 else 7
            pz = Pw[:, :] if half == 0 else Pout[:, :]

            def womm(g, sl=sl, o0=o0, pz=pz):
                g.matmul(pz, ones1b[0:1, :], row_bf["bo"][:, o0:o0 + 512],
                         start=True, stop=False)
                last = None
                for c in range(8):
                    last = g.matmul(
                        pz, attT[:, sl * D + 128 * c:sl * D + 128 * (c + 1)],
                        w_bf["wo"][:, D * c + o0:D * c + o0 + 512],
                        start=False, stop=(c == 7))
                return last
            E("pe", womm, reads=[b_attT[sl], b_w["wo"], b_row["bo"], b_ones1b],
              writes=[b_bank[bank]])
            E("dve", (lambda g, sl=sl, o0=o0, r3=r3, pz=pz: g.tensor_tensor(
                x2[:, sl * D + o0:sl * D + o0 + 512], pz,
                x_ring[:, D * r3 + o0:D * r3 + o0 + 512], ALU.add)),
              reads=[b_bank[bank], b_xr[r3]], writes=[b_x2[sl]])

    def S6a(t):
        sl = t % 2
        ln_pipeline(x2, b_x2[sl], sl * D, xn2_bf, b_xn2b[sl], sl * D,
                    16 * sl + 8, sl, b_stl2[sl])

    def S6b1(t):
        sl = t % 2
        transpose8(xn2_bf, b_xn2b[sl], sl * D, xn2T, b_xn2T[sl], sl * D, "act")

        def t1mm(g, sl=sl):
            last = None
            for c in range(8):
                last = g.matmul(Pout[0:32, 0:128], vr_bf[:, 32 * c:32 * (c + 1)],
                                xn2T[:, sl * D + 128 * c:sl * D + 128 * (c + 1)],
                                start=(c == 0), stop=(c == 7))
            return last
        E("pe", t1mm, reads=[b_xn2T[sl], b_vr], writes=[b_bank[7]])
        E("act", (lambda g, sl=sl: g.copy(t1b[:, 128 * sl:128 * (sl + 1)],
                                          Pout[0:32, 0:128])),
          reads=[b_bank[7]], writes=[b_t1b[sl]])

    def S6b2(t):
        sl = t % 2
        for half in range(2):
            o0 = 512 * half
            bank = 4 if half == 0 else 7
            pz = Pw[:, :] if half == 0 else Pout[:, :]

            def ffnmm(g, sl=sl, o0=o0, pz=pz):
                g.matmul(pz, ones1b[0:1, :], row_bf["bf"][:, o0:o0 + 512],
                         start=True, stop=False)
                for c in range(8):
                    g.matmul(pz, xn2T[:, sl * D + 128 * c:sl * D + 128 * (c + 1)],
                             w_bf["wf"][:, D * c + o0:D * c + o0 + 512],
                             start=False, stop=False)
                return g.matmul(pz, t1b[:, 128 * sl:128 * (sl + 1)],
                                ct_bf[:, o0:o0 + 512], start=False, stop=True)
            E("pe", ffnmm, reads=[b_xn2T[sl], b_w["wf"], b_t1b[sl], b_ct,
                                  b_row["bf"], b_ones1b],
              writes=[b_bank[bank]])
            E("dve", (lambda g, sl=sl, o0=o0, pz=pz: g.tensor_tensor(
                out_b[:, sl * D + o0:sl * D + o0 + 512], pz,
                x2[:, sl * D + o0:sl * D + o0 + 512], ALU.add)),
              reads=[b_bank[bank], b_x2[sl]], writes=[b_out[sl]])

        E("sp", (lambda g, t=t, sl=sl: g.dma_start(
            out=outd[T * t:T * (t + 1), :], in_=out_b[:, sl * D:sl * D + D])),
          reads=[b_out[sl]])

    S0(0)
    S0(1)
    S1a(0)
    S1b(0)
    for t in range(NT + 4):
        if t + 1 < NT:
            S1a(t + 1, part=0)
        if 0 <= t - 2 < NT:
            S4(t - 2)
        if t + 1 < NT:
            S1a(t + 1, part=1)
        if 0 <= t - 1 < NT:
            S3a(t - 1)
        if t - 3 >= 0 and t - 3 < NT:
            S5(t - 3)
        if t + 2 < NT:
            S0(t + 2)
        if t - 3 >= 0 and t - 3 < NT:
            S6a(t - 3)
        if t < NT:
            S2h(t, 0)
        if t - 4 >= 0:
            S6b1(t - 4)
        if t < NT:
            S2h(t, 1)
        if t - 4 >= 0:
            S6b2(t - 4)
        if 0 <= t - 1 < NT:
            S3b(t - 1)
        if t + 1 < NT:
            S1b(t + 1)

    total_sp = s.cnt["sp"]
    E("sp", lambda g: g.nop(), deps=[("sp", total_sp)])

    with (
        nc.semaphore("sem_pe") as sem_pe,
        nc.semaphore("sem_dve") as sem_dve,
        nc.semaphore("sem_act") as sem_act,
        nc.semaphore("sem_pool") as sem_pool,
        nc.semaphore("sem_sp") as sem_sp,
        nc.Block() as block,
    ):
        sems = {"pe": sem_pe, "dve": sem_dve, "act": sem_act,
                "pool": sem_pool, "sp": sem_sp}

        def make_body(name):
            def body(eng):
                for waits, fn in s.streams[name]:
                    for (pn, v) in waits:
                        eng.wait_ge(sems[pn], v)
                    fn(eng).then_inc(sems[name], s.step[name])
            return body

        block.tensor(make_body("pe"))
        block.vector(make_body("dve"))
        block.scalar(make_body("act"))
        block.gpsimd(make_body("pool"))
        block.sync(make_body("sp"))

    return nc


_NC_CACHE = None


def kernel(L0_attn_x, L0_attn_Wq, L0_attn_bq, L0_attn_Wk, L0_attn_bk,
           L0_attn_Wv, L0_attn_bv, L0_attn_Wo, L0_attn_bo,
           L0_attn_ln_q_w, L0_attn_ln_q_b, L0_attn_ln_k_w, L0_attn_ln_k_b,
           L0_ffn_W, L0_ffn_b, L0_ffn_V_r, L0_ffn_C):
    global _NC_CACHE
    x = np.asarray(L0_attn_x, dtype=np.float32)
    assert x.shape == (8, S, D)

    def t32(a):
        return np.ascontiguousarray(np.asarray(a, dtype=np.float32))

    common = {
        "wq_t": t32(np.asarray(L0_attn_Wq).T),
        "wk_t": t32(np.asarray(L0_attn_Wk).T),
        "wv_t": t32(np.asarray(L0_attn_Wv).T),
        "wo_t": t32(np.asarray(L0_attn_Wo).T),
        "wf_t": t32(np.asarray(L0_ffn_W).T),
        "vr": t32(L0_ffn_V_r),
        "ct": t32(np.asarray(L0_ffn_C).T),
        "r_bq": t32(L0_attn_bq)[None, :],
        "r_bk": t32(L0_attn_bk)[None, :],
        "r_bv": t32(L0_attn_bv)[None, :],
        "r_bo": t32(L0_attn_bo)[None, :],
        "r_bf": t32(L0_ffn_b)[None, :],
        "r_lnqw": np.ascontiguousarray(np.tile(t32(L0_attn_ln_q_w), NH))[None, :],
        "r_lnqb": np.ascontiguousarray(np.tile(t32(L0_attn_ln_q_b), NH))[None, :],
        "r_lnkw": np.ascontiguousarray(np.tile(t32(L0_attn_ln_k_w), NH))[None, :],
    }
    in_maps = []
    for b in range(8):
        m = dict(common)
        m["x"] = np.ascontiguousarray(x[b])
        in_maps.append(m)

    if _NC_CACHE is None:
        _NC_CACHE = build_nc()
    res = run_bass_kernel_spmd(_NC_CACHE, in_maps, core_ids=list(range(8)))
    out = np.stack([np.asarray(r["out"], dtype=np.float32) for r in res.results])
    return out


# version 36
# speedup vs baseline: 1.1946x; 1.0049x over previous
import os
import numpy as np
import concourse.bass as bass
import concourse.mybir as mybir
from concourse.bass_utils import run_bass_kernel_spmd

F32 = mybir.dt.float32
BF16 = mybir.dt.bfloat16
AF = mybir.ActivationFunctionType
ALU = mybir.AluOpType
AX = mybir.AxisListType

S = 4096
D = 1024
T = 128
NT = S // T
NH = 16
HD = 64
EPS = 1e-5

ENGINES = ("pe", "dve", "act", "pool", "sp")
SELF_WAIT = {"dve", "act", "pool", "sp"}


class Buf:
    __slots__ = ("name", "writers", "readers")

    def __init__(self, name):
        self.name = name
        self.writers = {}
        self.readers = {}


class Sched:
    def __init__(self):
        self.streams = {e: [] for e in ENGINES}
        self.cnt = {e: 0 for e in ENGINES}
        self.step = {e: (16 if e == "sp" else 1) for e in ENGINES}
        self.lastw = {}

    def emit(self, eng, fn, reads=(), writes=(), deps=()):
        need = {}

        def add(e, c):
            need[e] = max(need.get(e, 0), c)

        for b in reads:
            for e, c in b.writers.items():
                add(e, c)
        for b in writes:
            for e, c in b.writers.items():
                add(e, c)
            for e, c in b.readers.items():
                add(e, c)
        for e, c in deps:
            add(e, c)
        waits = []
        for e, c in sorted(need.items()):
            if e == eng and eng not in SELF_WAIT:
                continue
            k = (eng, e)
            if self.lastw.get(k, 0) >= c:
                continue
            self.lastw[k] = c
            waits.append((e, c * self.step[e]))
        self.cnt[eng] += 1
        me = self.cnt[eng]
        self.streams[eng].append((waits, fn))
        for b in reads:
            b.readers[eng] = max(b.readers.get(eng, 0), me)
        for b in writes:
            b.writers[eng] = max(b.writers.get(eng, 0), me)
        return (eng, me)


DBG = int(os.environ.get("KDBG", "0"))
DT = 2


def build_nc():
    nc = bass.Bass("TRN2", target_bir_lowering=False, dynamic_dma_scratch_size=4096)
    dbg = {}

    def dbg_out(name, shape, dtype):
        dbg[name] = nc.dram_tensor("dbg_" + name, shape, dtype,
                                   kind="ExternalOutput").ap()
        return dbg[name]

    xd = nc.dram_tensor("x", [S, D], F32, kind="ExternalInput").ap()
    outd = nc.dram_tensor("out", [S, D], F32, kind="ExternalOutput").ap()
    wdram = {}
    for nm in ("wq", "wk", "wv", "wo", "wf"):
        wdram[nm] = nc.dram_tensor(nm + "_t", [D, D], F32, kind="ExternalInput").ap()
    vr_d = nc.dram_tensor("vr", [D, 32], F32, kind="ExternalInput").ap()
    ct_d = nc.dram_tensor("ct", [32, D], F32, kind="ExternalInput").ap()
    rows_d = {}
    for nm in ("bq", "bk", "bv", "bo", "bf", "lnqw", "lnqb", "lnkw"):
        rows_d[nm] = nc.dram_tensor("r_" + nm, [1, D], F32, kind="ExternalInput").ap()

    s = Sched()
    E = s.emit

    def sb(name, shape, dtype):
        return nc.alloc_sbuf_tensor(name, shape, dtype)

    w_bf = {nm: sb(nm + "_bf", [128, 8 * D], BF16)
            for nm in ("wq", "wk", "wv", "wo", "wf")}
    vr_bf = sb("vr_bf", [128, 8 * 32], BF16)
    ct_bf = sb("ct_bf", [32, D], BF16)
    row_bf = {nm: sb("row_" + nm, [1, D], BF16)
              for nm in ("bq", "bk", "bv", "bo", "bf")}
    lnqw_rep = sb("lnqw_rep", [128, D], F32)
    lnqb_rep = sb("lnqb_rep", [128, D], F32)
    lnkw_rep = sb("lnkw_rep", [128, D], F32)
    idb = sb("idb", [128, 128], BF16)
    mcomb = sb("mcomb", [128, 256], BF16)
    ones1f = sb("ones1f", [1, 128], F32)
    ones1b = sb("ones1b", [1, 128], BF16)
    negB = sb("negB", [128, 1], F32)
    zero1 = sb("zero1", [128, 1], F32)
    eps5 = sb("eps5", [128, 1], F32)
    smallB = sb("smallB", [1, 8], F32)
    sq64 = sb("sq64", [1, 64], F32)

    x_ring = sb("x_ring", [128, 5 * D], F32)
    kT_ring = sb("kT_ring", [128, 4 * D], BF16)
    vaug_ring = sb("vaug_ring", [128, 4 * NH * 65], BF16)
    qT = sb("qT", [128, 2 * D], BF16)
    xnT = sb("xnT", [128, 2 * D], BF16)
    xn2T = sb("xn2T", [128, 2 * D], BF16)
    attT = sb("attT", [128, 2 * D], BF16)
    q_tm = sb("q_tm", [128, 2 * D], F32)
    k_tm = sb("k_tm", [128, 2 * D], F32)
    qh_bf = sb("qh_bf", [128, 2 * D], BF16)
    kh_bf = sb("kh_bf", [128, 2 * D], BF16)
    xn_bf = sb("xn_bf", [128, 2 * D], BF16)
    xn2_bf = sb("xn2_bf", [128, 2 * D], BF16)
    att_bf = sb("att_bf", [128, 2 * D], BF16)
    x2 = sb("x2", [128, 2 * D], F32)
    out_b = x2
    sq_buf = sb("sq_buf", [128, D], F32)
    sq_buf2 = sb("sq_buf2", [128, D], F32)
    probsT = sb("probsT", [128, 2 * 768], BF16)
    t1b = sb("t1b", [32, 2 * 128], BF16)
    stq = sb("stq", [128, 2 * 64], F32)
    stk = sb("stk", [128, 2 * 64], F32)
    stl = sb("stl", [128, 2 * 16], F32)
    rq = sb("rq", [128, 2 * 16], F32)
    rqt = sb("rqt", [128, 2 * 16], F32)

    wstage = x_ring
    rowtmp = out_b

    Pqkv = nc.alloc_psum_tensor("Pqkv", [128, 1536], F32)
    Pa = nc.alloc_psum_tensor("Pa", [128, 512], F32)
    Pw = nc.alloc_psum_tensor("Pw", [128, 512], F32)
    PtrA = nc.alloc_psum_tensor("PtrA", [128, 1024], BF16)
    PtrB = nc.alloc_psum_tensor("PtrB", [128, 1024], BF16)
    Pout = nc.alloc_psum_tensor("Pout", [128, 512], F32)

    b_w = {nm: Buf("w_" + nm) for nm in w_bf}
    b_vr, b_ct = Buf("vr"), Buf("ct")
    b_row = {nm: Buf("row_" + nm) for nm in row_bf}
    b_lnrep = {"lnqw": Buf("lnqw_rep"), "lnqb": Buf("lnqb_rep"),
               "lnkw": Buf("lnkw_rep")}
    b_idb, b_mask = Buf("idb"), Buf("mask")
    b_ones1f, b_ones1b = Buf("ones1f"), Buf("ones1b")
    b_negB, b_zero1, b_eps5 = Buf("negB"), Buf("zero1"), Buf("eps5")
    b_smallB, b_sq64 = Buf("smallB"), Buf("sq64")
    b_xr = [Buf(f"x{i}") for i in range(5)]
    b_kT = [Buf(f"kT{i}") for i in range(4)]
    b_va = [Buf(f"va{i}") for i in range(4)]
    b_qT = [Buf(f"qT{i}") for i in range(2)]
    b_xnT = [Buf(f"xnT{i}") for i in range(2)]
    b_xn2T = [Buf(f"xn2T{i}") for i in range(2)]
    b_attT = [Buf(f"attT{i}") for i in range(2)]
    b_qtm = [Buf(f"qtm{i}") for i in range(2)]
    b_ktm = [Buf(f"ktm{i}") for i in range(2)]
    b_qhb = [Buf(f"qhb{i}") for i in range(2)]
    b_khb = [Buf(f"khb{i}") for i in range(2)]
    b_xnb = [Buf(f"xnb{i}") for i in range(2)]
    b_xn2b = [Buf(f"xn2b{i}") for i in range(2)]
    b_atb = [Buf(f"atb{i}") for i in range(2)]
    b_x2 = [Buf(f"x2_{i}") for i in range(2)]
    b_out = b_x2
    b_sq = Buf("sq_buf")
    b_sq2 = Buf("sq_buf2")
    b_pT = [Buf(f"pT{i}") for i in range(2)]
    b_t1b = [Buf(f"t1b{i}") for i in range(2)]
    b_stq = [Buf(f"stq{i}") for i in range(2)]
    b_stk = [Buf(f"stk{i}") for i in range(2)]
    b_stl = [Buf(f"stl{i}") for i in range(2)]
    b_stl2 = [Buf(f"stl2_{i}") for i in range(2)]
    b_rq = [Buf(f"rq{i}") for i in range(2)]
    b_rqt = [Buf(f"rqt{i}") for i in range(2)]
    b_bank = [Buf(f"bank{i}") for i in range(8)]
    b_wstage = b_xr[0]
    b_rowtmp = b_out[0]

    def initconsts(g):
        g.memset(idb[:, :], 0.0)
        g.memset(mcomb[:, :], 1.0)
        g.memset(ones1f[:, :], 1.0)
        g.memset(ones1b[:, :], 1.0)
        g.memset(zero1[:, :], 0.0)
        return g.memset(eps5[:, :], EPS)

    E("pool", initconsts,
      writes=[b_idb, b_mask, b_ones1f, b_ones1b, b_zero1, b_eps5])
    E("pool", lambda g: g.affine_select(
        out=idb[:, :], in_=idb[:, :], compare_op=ALU.not_equal,
        fill=1.0, base=0, pattern=[[-1, 128]], channel_multiplier=1),
      writes=[b_idb])
    E("pool", lambda g: g.affine_select(
        out=mcomb[:, 0:128], in_=mcomb[:, 0:128], compare_op=ALU.is_ge,
        fill=0.0, base=0, pattern=[[-1, 128]], channel_multiplier=1),
      writes=[b_mask])
    E("pool", lambda g: g.affine_select(
        out=mcomb[:, 128:256], in_=mcomb[:, 128:256], compare_op=ALU.is_ge,
        fill=0.0, base=0, pattern=[[1, 128]], channel_multiplier=-1),
      writes=[b_mask])

    def initvaug(g):
        last = None
        for slot in range(4):
            base = slot * NH * 65
            for h in range(NH):
                last = g.memset(
                    vaug_ring[:, base + 65 * h + 64:base + 65 * h + 65], 1.0)
        return last

    E("pool", initvaug, writes=[b_va[0], b_va[1], b_va[2], b_va[3]])

    for nm in ("wq", "wk", "wv", "wo", "wf"):
        for c in range(8):
            E("sp", (lambda g, nm=nm, c=c:
                     g.dma_start(out=wstage[:, 0:D],
                                 in_=wdram[nm][128 * c:128 * (c + 1), :])),
              writes=[b_wstage])
            E("dve", (lambda g, nm=nm, c=c:
                      g.tensor_copy(w_bf[nm][:, D * c:D * (c + 1)],
                                    wstage[:, 0:D])),
              reads=[b_wstage], writes=[b_w[nm]])

    E("sp", lambda g: g.dma_start(
        out=wstage[:, 0:256].rearrange("p (c r) -> p c r", c=8),
        in_=vr_d.rearrange("(c p) r -> p c r", p=128)),
      writes=[b_wstage])
    E("dve", lambda g: g.tensor_copy(vr_bf[:, :], wstage[:, 0:256]),
      reads=[b_wstage], writes=[b_vr])

    E("sp", lambda g: g.dma_start(out=wstage[0:32, 0:D], in_=ct_d),
      writes=[b_wstage])
    E("dve", lambda g: g.tensor_copy(ct_bf[:, :], wstage[0:32, 0:D]),
      reads=[b_wstage], writes=[b_ct])

    for nm in ("bq", "bk", "bv", "bo", "bf"):
        E("sp", (lambda g, nm=nm: g.dma_start(out=rowtmp[0:1, 0:D],
                                              in_=rows_d[nm])),
          writes=[b_rowtmp])
        E("dve", (lambda g, nm=nm: g.tensor_copy(row_bf[nm][:, :],
                                                 rowtmp[0:1, 0:D])),
          reads=[b_rowtmp], writes=[b_row[nm]])

    for nm, rep, brep, scale in (("lnqw", lnqw_rep, b_lnrep["lnqw"], 1.0),
                                 ("lnqb", lnqb_rep, b_lnrep["lnqb"], 1.0),
                                 ("lnkw", lnkw_rep, b_lnrep["lnkw"], 0.125)):
        E("sp", (lambda g, nm=nm: g.dma_start(out=rowtmp[0:1, 0:D],
                                              in_=rows_d[nm])),
          writes=[b_rowtmp])
        for half in range(2):
            E("pe", (lambda g, half=half:
                     g.matmul(Pout[:, 0:512], ones1f[0:1, :],
                              rowtmp[0:1, 512 * half:512 * (half + 1)],
                              start=True, stop=True)),
              reads=[b_ones1f, b_rowtmp], writes=[b_bank[7]])
            E("act", (lambda g, rep=rep, half=half, scale=scale:
                      g.mul(rep[:, 512 * half:512 * (half + 1)],
                            Pout[:, 0:512], scale)),
              reads=[b_bank[7]], writes=[brep])
        if nm == "lnqw":
            E("dve", lambda g: g.tensor_reduce(
                smallB[0:1, 0:1], rowtmp[0:1, 0:64], AX.X, ALU.max,
                apply_absolute_value=True),
              reads=[b_rowtmp], writes=[b_smallB])
        elif nm == "lnkw":
            E("dve", lambda g: g.tensor_reduce(
                smallB[0:1, 1:2], rowtmp[0:1, 0:64], AX.X, ALU.max,
                apply_absolute_value=True),
              reads=[b_rowtmp], writes=[b_smallB])
        else:
            E("dve", lambda g: g.tensor_tensor(sq64[0:1, :], rowtmp[0:1, 0:64],
                                               rowtmp[0:1, 0:64], ALU.mult),
              reads=[b_rowtmp], writes=[b_sq64])
            E("dve", lambda g: g.tensor_reduce(smallB[0:1, 2:3], sq64[0:1, :],
                                               AX.X, ALU.add),
              reads=[b_sq64], writes=[b_smallB])

    E("pool", lambda g: g.memset(smallB[0:1, 7:8], 1e-30), writes=[b_smallB])
    E("dve", lambda g: g.tensor_tensor(smallB[0:1, 3:4], smallB[0:1, 2:3],
                                       smallB[0:1, 7:8], ALU.add),
      reads=[b_smallB], writes=[b_smallB])
    E("act", lambda g: g.activation(smallB[0:1, 4:5], smallB[0:1, 3:4], AF.Ln,
                                    bias=zero1[0:1, :], scale=1.0),
      reads=[b_smallB, b_zero1], writes=[b_smallB])
    E("act", lambda g: g.activation(smallB[0:1, 4:5], smallB[0:1, 4:5], AF.Exp,
                                    bias=zero1[0:1, :], scale=0.5),
      reads=[b_smallB, b_zero1], writes=[b_smallB])
    E("dve", lambda g: g.tensor_scalar(smallB[0:1, 5:6], smallB[0:1, 0:1],
                                       8.0, None, ALU.mult),
      reads=[b_smallB], writes=[b_smallB])
    E("dve", lambda g: g.tensor_tensor(smallB[0:1, 5:6], smallB[0:1, 5:6],
                                       smallB[0:1, 4:5], ALU.add),
      reads=[b_smallB], writes=[b_smallB])
    E("dve", lambda g: g.tensor_tensor(smallB[0:1, 5:6], smallB[0:1, 5:6],
                                       smallB[0:1, 1:2], ALU.mult),
      reads=[b_smallB], writes=[b_smallB])
    E("dve", lambda g: g.tensor_scalar(smallB[0:1, 6:7], smallB[0:1, 5:6],
                                       -80.0, 0.0, ALU.add, ALU.max),
      reads=[b_smallB], writes=[b_smallB])
    E("pe", lambda g: g.matmul(Pout[:, 0:1], ones1f[0:1, :], smallB[0:1, 6:7],
                               start=True, stop=True),
      reads=[b_ones1f, b_smallB], writes=[b_bank[7]])
    E("dve", lambda g: g.tensor_scalar(negB[:, :], Pout[:, 0:1], -1.0, None,
                                       ALU.mult),
      reads=[b_bank[7]], writes=[b_negB])

    tr_state = [0]

    def transpose8(src, b_src, src_off, dstT, b_dstT, dst_off, evac_eng="dve"):
        par = tr_state[0] % 2
        tr_state[0] += 1
        P = PtrA if par == 0 else PtrB
        pb = b_bank[5 + par]

        def trs(g):
            last = None
            for c in range(8):
                last = g.transpose(
                    P[:, 128 * c:128 * (c + 1)],
                    src[:, src_off + 128 * c:src_off + 128 * (c + 1)],
                    idb[:, :])
            return last
        E("pe", trs, reads=[b_src, b_idb], writes=[pb])
        if evac_eng == "dve":
            E("dve", (lambda g: g.tensor_copy(dstT[:, dst_off:dst_off + D],
                                              P[:, :])),
              reads=[pb], writes=[b_dstT])
        else:
            E("act", (lambda g: g.copy(dstT[:, dst_off:dst_off + D], P[:, :])),
              reads=[pb], writes=[b_dstT])

    def ln_pipeline(src, b_src, src_off, dst, b_dst, dst_off, st_off, sl,
                    bstl, part=None):
        if part != 1:
            E("dve", (lambda g: g.tensor_scalar(
                dst[:, dst_off:dst_off + D], src[:, src_off:src_off + D], 1.0,
                0.0, ALU.mult, ALU.add,
                accum_out=stl[:, st_off:st_off + 1])),
              reads=[b_src], writes=[bstl, b_dst])
        if part == 0:
            return
        E("act", (lambda g: g.activation(
            dst[:, dst_off:dst_off + D], src[:, src_off:src_off + D], AF.Square,
            bias=zero1[:, :], scale=1.0,
            accum_out=stl[:, st_off + 1:st_off + 2])),
          reads=[b_src, b_zero1], writes=[bstl, b_dst])
        E("dve", (lambda g: g.tensor_scalar(
            stl[:, st_off + 2:st_off + 3], stl[:, st_off:st_off + 1],
            1.0 / D, None, ALU.mult)), reads=[bstl], writes=[bstl])
        E("dve", (lambda g: g.tensor_tensor(
            stl[:, st_off + 3:st_off + 4], stl[:, st_off + 2:st_off + 3],
            stl[:, st_off + 2:st_off + 3], ALU.mult)),
          reads=[bstl], writes=[bstl])
        E("dve", (lambda g: g.scalar_tensor_tensor(
            stl[:, st_off + 3:st_off + 4], stl[:, st_off + 3:st_off + 4],
            -float(D), stl[:, st_off + 1:st_off + 2], ALU.mult, ALU.add)),
          reads=[bstl], writes=[bstl])
        E("act", (lambda g: g.activation(
            stl[:, st_off + 4:st_off + 5], stl[:, st_off + 3:st_off + 4],
            AF.Ln, bias=eps5[:, :], scale=1.0 / D)),
          reads=[bstl, b_eps5], writes=[bstl])
        E("act", (lambda g: g.activation(
            stl[:, st_off + 4:st_off + 5], stl[:, st_off + 4:st_off + 5],
            AF.Exp, bias=zero1[:, :], scale=-0.5)),
          reads=[bstl, b_zero1], writes=[bstl])
        E("dve", (lambda g: g.tensor_scalar(
            dst[:, dst_off:dst_off + D], src[:, src_off:src_off + D],
            stl[:, st_off + 2:st_off + 3], stl[:, st_off + 4:st_off + 5],
            ALU.subtract, ALU.mult)),
          reads=[b_src, bstl], writes=[b_dst])

    def head_ln(src_t, b_src, so, dst, b_dst, do, st, b_st, sto,
                gain_rep, b_gain, bias_rep, b_bias, stage, b_mysq):
        mysq = sq_buf if b_mysq is b_sq else sq_buf2
        if stage == 0:
            E("act", (lambda g: g.activation(mysq[:, 0:D], src_t[:, so:so + D],
                                             AF.Square, bias=zero1[:, :],
                                             scale=1.0)),
              reads=[b_src, b_zero1], writes=[b_mysq])
            E("dve", (lambda g: g.tensor_reduce(
                st[:, sto:sto + 16],
                src_t[:, so:so + D].rearrange("p (g c) -> p g c", g=16),
                AX.X, ALU.add)),
              reads=[b_src], writes=[b_st])
        elif stage == 1:
            E("dve", (lambda g: g.tensor_reduce(
                st[:, sto + 16:sto + 32],
                mysq[:, 0:D].rearrange("p (g c) -> p g c", g=16),
                AX.X, ALU.add)),
              reads=[b_mysq], writes=[b_st])
            E("dve", (lambda g: g.tensor_scalar(st[:, sto + 32:sto + 48],
                                                st[:, sto:sto + 16], 1.0 / HD,
                                                None, ALU.mult)),
              reads=[b_st], writes=[b_st])
            E("dve", (lambda g: g.tensor_tensor(st[:, sto + 48:sto + 64],
                                                st[:, sto + 32:sto + 48],
                                                st[:, sto + 32:sto + 48],
                                                ALU.mult)),
              reads=[b_st], writes=[b_st])
            E("dve", (lambda g: g.scalar_tensor_tensor(
                st[:, sto + 48:sto + 64], st[:, sto + 48:sto + 64], -float(HD),
                st[:, sto + 16:sto + 32], ALU.mult, ALU.add)),
              reads=[b_st], writes=[b_st])
        elif stage == 2:
            E("act", (lambda g: g.activation(st[:, sto + 48:sto + 64],
                                             st[:, sto + 48:sto + 64], AF.Ln,
                                             bias=eps5[:, :], scale=1.0 / HD)),
              reads=[b_st, b_eps5], writes=[b_st])
            E("act", (lambda g: g.activation(st[:, sto + 48:sto + 64],
                                             st[:, sto + 48:sto + 64], AF.Exp,
                                             bias=zero1[:, :], scale=-0.5)),
              reads=[b_st, b_zero1], writes=[b_st])
        elif stage == 3:
            def hseg(g):
                last = None
                for h in range(NH):
                    last = g.tensor_scalar(
                        src_t[:, so + HD * h:so + HD * (h + 1)],
                        src_t[:, so + HD * h:so + HD * (h + 1)],
                        st[:, sto + 32 + h:sto + 33 + h],
                        st[:, sto + 48 + h:sto + 49 + h],
                        ALU.subtract, ALU.mult)
                return last
            E("dve", hseg, reads=[b_st], writes=[b_src])
        elif stage == 4:
            if bias_rep is not None:
                E("pool", (lambda g: g.tensor_tensor(src_t[:, so:so + D],
                                                     src_t[:, so:so + D],
                                                     gain_rep[:, :], ALU.mult)),
                  reads=[b_gain], writes=[b_src])
                E("pool", (lambda g: g.tensor_tensor(dst[:, do:do + D],
                                                     src_t[:, so:so + D],
                                                     bias_rep[:, :], ALU.add)),
                  reads=[b_src, b_bias], writes=[b_dst])
            else:
                E("pool", (lambda g: g.tensor_tensor(dst[:, do:do + D],
                                                     src_t[:, so:so + D],
                                                     gain_rep[:, :], ALU.mult)),
                  reads=[b_src, b_gain], writes=[b_dst])

    def blocks_of(t):
        if t == 0:
            return [(0, "c")]
        if t == 1:
            return [(0, None), (1, "c")]
        return [(t - 2, "w"), (t - 1, None), (t, "c")]

    def pacol(h):
        if h < 14:
            return 65 * (h % 7)
        return 65 * (h - 14)

    def pabank(h):
        return 3 if (h < 7 or h >= 14) else 4

    def S0(t):
        r3 = t % 5
        E("sp", (lambda g, t=t, r3=r3: g.dma_start(
            out=x_ring[:, D * r3:D * (r3 + 1)], in_=xd[T * t:T * (t + 1), :])),
          writes=[b_xr[r3]])

    def S1a(t, part=None):
        sl = t % 2
        r3 = t % 5
        ln_pipeline(x_ring, b_xr[r3], D * r3, xn_bf, b_xnb[sl], sl * D,
                    16 * sl, sl, b_stl[sl], part=part)

    def S1b(t):
        sl = t % 2
        transpose8(xn_bf, b_xnb[sl], sl * D, xnT, b_xnT[sl], sl * D, "act")

    def S2h(t, half):
        sl = t % 2
        r4 = t % 4
        if True:
            o0 = 512 * half

            def qkvmm(g, sl=sl, o0=o0):
                last = None
                for ti, (wname, rowname) in enumerate(
                        (("wq", "bq"), ("wk", "bk"), ("wv", "bv"))):
                    pz = Pqkv[:, 512 * ti:512 * (ti + 1)]
                    g.matmul(pz, ones1b[0:1, :], row_bf[rowname][:, o0:o0 + 512],
                             start=True, stop=False)
                    for c in range(8):
                        last = g.matmul(
                            pz, xnT[:, sl * D + 128 * c:sl * D + 128 * (c + 1)],
                            w_bf[wname][:, D * c + o0:D * c + o0 + 512],
                            start=False, stop=(c == 7))
                return last
            E("pe", qkvmm,
              reads=[b_xnT[sl], b_w["wq"], b_w["wk"], b_w["wv"], b_row["bq"],
                     b_row["bk"], b_row["bv"], b_ones1b],
              writes=[b_bank[0], b_bank[1], b_bank[2]])

            E("act", (lambda g, sl=sl, o0=o0: g.copy(
                q_tm[:, sl * D + o0:sl * D + o0 + 512], Pqkv[:, 0:512])),
              reads=[b_bank[0]], writes=[b_qtm[sl]])
            E("act", (lambda g, sl=sl, o0=o0: g.copy(
                k_tm[:, sl * D + o0:sl * D + o0 + 512], Pqkv[:, 512:1024])),
              reads=[b_bank[1]], writes=[b_ktm[sl]])

            def vev(g, r4=r4, half=half):
                base = r4 * NH * 65 + 65 * 8 * half
                outap = bass.AP(vaug_ring, base,
                                [[4 * NH * 65, 128], [65, 8], [1, 64]])
                return g.tensor_copy(
                    outap, Pqkv[:, 1024:1536].rearrange("p (g c) -> p g c", g=8))
            E("dve", vev, reads=[b_bank[2]], writes=[b_va[r4]])

    def S3a(t):
        sl = t % 2
        for stage in range(5):
            head_ln(q_tm, b_qtm[sl], sl * D, qh_bf, b_qhb[sl], sl * D,
                    stq, b_stq[sl], sl * 64,
                    lnqw_rep, b_lnrep["lnqw"], lnqb_rep, b_lnrep["lnqb"],
                    stage, b_sq)
            head_ln(k_tm, b_ktm[sl], sl * D, kh_bf, b_khb[sl], sl * D,
                    stk, b_stk[sl], sl * 64,
                    lnkw_rep, b_lnrep["lnkw"], None, None,
                    stage, b_sq2)

    def S3b(t):
        sl = t % 2
        r4 = t % 4
        transpose8(qh_bf, b_qhb[sl], sl * D, qT, b_qT[sl], sl * D, "act")
        transpose8(kh_bf, b_khb[sl], sl * D, kT_ring, b_kT[r4], r4 * D, "act")

    def S4(t):
        sl = t % 2
        blocks = blocks_of(t)
        nb = len(blocks)

        def emit_qkt(h):
            hp = 64 * (h % 2)
            hc = 128 * (h // 2)
            scs = h % 3

            def qkt(g, sl=sl, hp=hp, hc=hc, scs=scs, blocks=blocks):
                last = None
                for ci, (tk, _m) in enumerate(blocks):
                    last = g.matmul(
                        Pqkv[:, 512 * scs + 128 * ci:512 * scs + 128 * (ci + 1)],
                        kT_ring[hp:hp + 64,
                                (tk % 4) * D + hc:(tk % 4) * D + hc + 128],
                        qT[hp:hp + 64, sl * D + hc:sl * D + hc + 128],
                        start=(ci == 0), stop=(ci == len(blocks) - 1),
                        tile_position=(hp, 0))
                return last
            kreads = [b_kT[tk % 4] for tk, _m in blocks]
            E("pe", qkt, reads=[b_qT[sl]] + kreads, writes=[b_bank[scs]])

        def emit_expmask(p):
            psl = p % 2
            scs0 = (2 * p) % 3
            scs1 = (2 * p + 1) % 3
            step = 512 * (scs1 - scs0)
            po = 768 * psl
            if step > 0:
                def pexp(g, scs0=scs0, step=step, po=po, nb=nb):
                    iap = bass.AP(Pqkv, 512 * scs0,
                                  [[1536, 128], [step, 2], [1, 128 * nb]])
                    oap = bass.AP(probsT, po,
                                  [[1536, 128], [384, 2], [1, 128 * nb]])
                    return g.activation(oap, iap, AF.Exp, bias=negB[:, :],
                                        scale=1.0)
                E("act", pexp, reads=[b_bank[scs0], b_bank[scs1], b_negB],
                  writes=[b_pT[psl]])
            else:
                for i, scs in enumerate((scs0, scs1)):
                    E("act", (lambda g, scs=scs, po=po + 384 * i, nb=nb:
                              g.activation(probsT[:, po:po + 128 * nb],
                                           Pqkv[:, 512 * scs:512 * scs + 128 * nb],
                                           AF.Exp, bias=negB[:, :], scale=1.0)),
                      reads=[b_bank[scs], b_negB], writes=[b_pT[psl]])
            masked = [(ci, m) for ci, (tk, m) in enumerate(blocks) if m]
            for i in range(2):
                meng = "dve"
                hpo = po + 384 * i
                if len(masked) == 2:
                    def hmask(g, hpo=hpo, mlo=masked[0][0]):
                        off = hpo + 128 * mlo
                        io = bass.AP(probsT, off,
                                     [[1536, 128], [256, 2], [1, 128]])
                        mo = bass.AP(mcomb, 0, [[256, 128], [128, 2], [1, 128]])
                        return g.tensor_tensor(io, io, mo, ALU.mult)
                    E(meng, hmask, reads=[b_mask], writes=[b_pT[psl]])
                else:
                    for ci, m in masked:
                        moff = 0 if m == "w" else 128
                        E(meng, (lambda g, hpo=hpo, ci=ci, moff=moff:
                                 g.tensor_tensor(
                                     probsT[:, hpo + 128 * ci:hpo + 128 * (ci + 1)],
                                     probsT[:, hpo + 128 * ci:hpo + 128 * (ci + 1)],
                                     mcomb[:, moff:moff + 128], ALU.mult)),
                          reads=[b_mask], writes=[b_pT[psl]])

        def emit_pv(h):
            psl = (h // 2) % 2
            po = 768 * psl + 384 * (h % 2)

            def pv(g, h=h, po=po, blocks=blocks):
                pc = pacol(h)
                Pz = Pa if pabank(h) == 3 else Pw
                last = None
                for ci, (tk, _m) in enumerate(blocks):
                    last = g.matmul(
                        Pz[:, pc:pc + 65],
                        probsT[:, po + 128 * ci:po + 128 * (ci + 1)],
                        vaug_ring[:, (tk % 4) * NH * 65 + 65 * h:
                                  (tk % 4) * NH * 65 + 65 * (h + 1)],
                        start=(ci == 0 and h in (0, 7, 14)),
                        stop=(ci == len(blocks) - 1 and h in (6, 13, 15)))
                return last
            vreads = [b_va[tk % 4] for tk, _m in blocks]
            E("pe", pv, reads=[b_pT[psl]] + vreads, writes=[b_bank[pabank(h)]])

            if h in (6, 13, 15):
                lo = {6: 0, 13: 7, 15: 14}[h]
                n = h - lo + 1

                Pz2 = Pa if pabank(h) == 3 else Pw
                E("dve", (lambda g, lo=lo, n=n, sl=sl, Pz2=Pz2: g.tensor_scalar(
                    rqt[:, 16 * sl + lo:16 * sl + lo + n],
                    bass.AP(Pz2, pacol(lo) + 64, [[512, 128], [65, n]]),
                    1e-38, None, ALU.add)),
                  reads=[b_bank[pabank(h)]], writes=[b_rqt[sl]])
                E("dve", (lambda g, lo=lo, n=n, sl=sl: g.reciprocal(
                    rq[:, 16 * sl + lo:16 * sl + lo + n],
                    rqt[:, 16 * sl + lo:16 * sl + lo + n])),
                  reads=[b_rqt[sl]], writes=[b_rq[sl]])

                def gevac(g, lo=lo, n=n, sl=sl, Pz2=Pz2):
                    last = None
                    for hh in range(lo, lo + n):
                        last = g.tensor_scalar(
                            att_bf[:, sl * D + HD * hh:sl * D + HD * (hh + 1)],
                            Pz2[:, pacol(hh):pacol(hh) + 64],
                            rq[:, 16 * sl + hh:16 * sl + hh + 1], None,
                            ALU.mult)
                    return last
                E("dve", gevac, reads=[b_bank[pabank(h)], b_rq[sl]],
                  writes=[b_atb[sl]])

        for p in range(8):
            emit_qkt(2 * p)
            emit_qkt(2 * p + 1)
            emit_expmask(p)
            if p >= 1:
                emit_pv(2 * p - 2)
                emit_pv(2 * p - 1)
        emit_pv(14)
        emit_pv(15)

    def S5(t):
        sl = t % 2
        r3 = t % 5
        transpose8(att_bf, b_atb[sl], sl * D, attT, b_attT[sl], sl * D, "act")
        for half in range(2):
            o0 = 512 * half
            bank = 4 if half == 0 else 7
            pz = Pw[:, :] if half == 0 else Pout[:, :]

            def womm(g, sl=sl, o0=o0, pz=pz):
                g.matmul(pz, ones1b[0:1, :], row_bf["bo"][:, o0:o0 + 512],
                         start=True, stop=False)
                last = None
                for c in range(8):
                    last = g.matmul(
                        pz, attT[:, sl * D + 128 * c:sl * D + 128 * (c + 1)],
                        w_bf["wo"][:, D * c + o0:D * c + o0 + 512],
                        start=False, stop=(c == 7))
                return last
            E("pe", womm, reads=[b_attT[sl], b_w["wo"], b_row["bo"], b_ones1b],
              writes=[b_bank[bank]])
            E("dve", (lambda g, sl=sl, o0=o0, r3=r3, pz=pz: g.tensor_tensor(
                x2[:, sl * D + o0:sl * D + o0 + 512], pz,
                x_ring[:, D * r3 + o0:D * r3 + o0 + 512], ALU.add)),
              reads=[b_bank[bank], b_xr[r3]], writes=[b_x2[sl]])

    def S6a(t):
        sl = t % 2
        ln_pipeline(x2, b_x2[sl], sl * D, xn2_bf, b_xn2b[sl], sl * D,
                    16 * sl + 8, sl, b_stl2[sl])

    def S6b1(t):
        sl = t % 2
        transpose8(xn2_bf, b_xn2b[sl], sl * D, xn2T, b_xn2T[sl], sl * D, "act")

        def t1mm(g, sl=sl):
            last = None
            for c in range(8):
                last = g.matmul(Pout[0:32, 0:128], vr_bf[:, 32 * c:32 * (c + 1)],
                                xn2T[:, sl * D + 128 * c:sl * D + 128 * (c + 1)],
                                start=(c == 0), stop=(c == 7))
            return last
        E("pe", t1mm, reads=[b_xn2T[sl], b_vr], writes=[b_bank[7]])
        E("act", (lambda g, sl=sl: g.copy(t1b[:, 128 * sl:128 * (sl + 1)],
                                          Pout[0:32, 0:128])),
          reads=[b_bank[7]], writes=[b_t1b[sl]])

    def S6b2(t):
        sl = t % 2
        for half in range(2):
            o0 = 512 * half
            bank = 4 if half == 0 else 7
            pz = Pw[:, :] if half == 0 else Pout[:, :]

            def ffnmm(g, sl=sl, o0=o0, pz=pz):
                g.matmul(pz, ones1b[0:1, :], row_bf["bf"][:, o0:o0 + 512],
                         start=True, stop=False)
                for c in range(8):
                    g.matmul(pz, xn2T[:, sl * D + 128 * c:sl * D + 128 * (c + 1)],
                             w_bf["wf"][:, D * c + o0:D * c + o0 + 512],
                             start=False, stop=False)
                return g.matmul(pz, t1b[:, 128 * sl:128 * (sl + 1)],
                                ct_bf[:, o0:o0 + 512], start=False, stop=True)
            E("pe", ffnmm, reads=[b_xn2T[sl], b_w["wf"], b_t1b[sl], b_ct,
                                  b_row["bf"], b_ones1b],
              writes=[b_bank[bank]])
            E("dve", (lambda g, sl=sl, o0=o0, pz=pz: g.tensor_tensor(
                out_b[:, sl * D + o0:sl * D + o0 + 512], pz,
                x2[:, sl * D + o0:sl * D + o0 + 512], ALU.add)),
              reads=[b_bank[bank], b_x2[sl]], writes=[b_out[sl]])

        E("sp", (lambda g, t=t, sl=sl: g.dma_start(
            out=outd[T * t:T * (t + 1), :], in_=out_b[:, sl * D:sl * D + D])),
          reads=[b_out[sl]])

    S0(0)
    S0(1)
    S1a(0)
    S1b(0)
    for t in range(NT + 4):
        if t + 1 < NT:
            S1a(t + 1, part=0)
        if 0 <= t - 2 < NT:
            S4(t - 2)
        if t + 1 < NT:
            S1a(t + 1, part=1)
        if 0 <= t - 1 < NT:
            S3a(t - 1)
        if 0 <= t - 3 < NT:
            S5(t - 3)
        if t + 2 < NT:
            S0(t + 2)
        if 0 <= t - 3 < NT:
            S6a(t - 3)
        if t < NT:
            S2h(t, 0)
        if 0 <= t - 1 < NT:
            S3b(t - 1)
        if t - 4 >= 0:
            S6b1(t - 4)
        if t < NT:
            S2h(t, 1)
        if t - 4 >= 0:
            S6b2(t - 4)
        if t + 1 < NT:
            S1b(t + 1)

    total_sp = s.cnt["sp"]
    E("sp", lambda g: g.nop(), deps=[("sp", total_sp)])

    with (
        nc.semaphore("sem_pe") as sem_pe,
        nc.semaphore("sem_dve") as sem_dve,
        nc.semaphore("sem_act") as sem_act,
        nc.semaphore("sem_pool") as sem_pool,
        nc.semaphore("sem_sp") as sem_sp,
        nc.Block() as block,
    ):
        sems = {"pe": sem_pe, "dve": sem_dve, "act": sem_act,
                "pool": sem_pool, "sp": sem_sp}

        def make_body(name):
            def body(eng):
                for waits, fn in s.streams[name]:
                    for (pn, v) in waits:
                        eng.wait_ge(sems[pn], v)
                    fn(eng).then_inc(sems[name], s.step[name])
            return body

        block.tensor(make_body("pe"))
        block.vector(make_body("dve"))
        block.scalar(make_body("act"))
        block.gpsimd(make_body("pool"))
        block.sync(make_body("sp"))

    return nc


_NC_CACHE = None


def kernel(L0_attn_x, L0_attn_Wq, L0_attn_bq, L0_attn_Wk, L0_attn_bk,
           L0_attn_Wv, L0_attn_bv, L0_attn_Wo, L0_attn_bo,
           L0_attn_ln_q_w, L0_attn_ln_q_b, L0_attn_ln_k_w, L0_attn_ln_k_b,
           L0_ffn_W, L0_ffn_b, L0_ffn_V_r, L0_ffn_C):
    global _NC_CACHE
    x = np.asarray(L0_attn_x, dtype=np.float32)
    assert x.shape == (8, S, D)

    def t32(a):
        return np.ascontiguousarray(np.asarray(a, dtype=np.float32))

    common = {
        "wq_t": t32(np.asarray(L0_attn_Wq).T),
        "wk_t": t32(np.asarray(L0_attn_Wk).T),
        "wv_t": t32(np.asarray(L0_attn_Wv).T),
        "wo_t": t32(np.asarray(L0_attn_Wo).T),
        "wf_t": t32(np.asarray(L0_ffn_W).T),
        "vr": t32(L0_ffn_V_r),
        "ct": t32(np.asarray(L0_ffn_C).T),
        "r_bq": t32(L0_attn_bq)[None, :],
        "r_bk": t32(L0_attn_bk)[None, :],
        "r_bv": t32(L0_attn_bv)[None, :],
        "r_bo": t32(L0_attn_bo)[None, :],
        "r_bf": t32(L0_ffn_b)[None, :],
        "r_lnqw": np.ascontiguousarray(np.tile(t32(L0_attn_ln_q_w), NH))[None, :],
        "r_lnqb": np.ascontiguousarray(np.tile(t32(L0_attn_ln_q_b), NH))[None, :],
        "r_lnkw": np.ascontiguousarray(np.tile(t32(L0_attn_ln_k_w), NH))[None, :],
    }
    in_maps = []
    for b in range(8):
        m = dict(common)
        m["x"] = np.ascontiguousarray(x[b])
        in_maps.append(m)

    if _NC_CACHE is None:
        _NC_CACHE = build_nc()
    res = run_bass_kernel_spmd(_NC_CACHE, in_maps, core_ids=list(range(8)))
    out = np.stack([np.asarray(r["out"], dtype=np.float32) for r in res.results])
    return out
